# version 17
# baseline (speedup 1.0000x reference)
import contextlib
import numpy as np
import ml_dtypes
import concourse.bass as bass
import concourse.mybir as mybir
from concourse.bass_utils import run_bass_kernel_spmd

F32 = mybir.dt.float32
BF16 = mybir.dt.bfloat16
AF = mybir.ActivationFunctionType
ALU = mybir.AluOpType
AX = mybir.AxisListType

D, FCD, T = 1024, 8, 2048
C, NCH, H, HD = 128, 16, 16, 64
HIN, HPAD, KCI = 2730, 2816, 22
DH, FH = 4096, 32
EPS = 1e-5
TH = 1024
NHALF = 2
ENGINES = ("tensor", "vector", "scalar", "sync", "gpsimd")

RW_M1, RW_R1, RW_S12, RW_MZ, RW_SZ, RW_MY, RW_RY, RW_TMP = range(8)
RET_LEVEL = 5


class Sem:
    __slots__ = ("h", "n", "name")

    def __init__(self, h, name):
        self.h = h
        self.n = 0
        self.name = name


class KB:
    def __init__(self, nc, es):
        self.nc = nc
        self.es = es
        self.ops = {e: [] for e in ENGINES}
        self.nsem = 0
        self.store_sem = None
        self.extra_barrier_waits = None
        self.job = None

    def pump(self):
        if self.job is not None:
            try:
                next(self.job)
            except StopIteration:
                self.job = None

    def drain_job(self):
        while self.job is not None:
            self.pump()

    def sem(self, name=None):
        name = name or f"s{self.nsem}"
        self.nsem += 1
        h = self.es.enter_context(self.nc.semaphore(name))
        return Sem(h, name)

    def op(self, eng, fn, waits=(), inc=None):
        ws = tuple((s.h, int(v)) for (s, v) in waits if v > 0)
        if inc is not None:
            inc[0].n += inc[1]
            inc = (inc[0].h, inc[1])
        self.ops[eng].append((fn, ws, inc))

    def drain(self, eng, waits=(), inc=None):
        self.op(eng, lambda e: e.drain(), waits=waits, inc=inc)

    def barrier(self, tag):
        if not hasattr(self, "_bar"):
            self._bar = {e: self.sem(f"bar_{e}") for e in ENGINES}
        sems = self._bar
        for e in ENGINES:
            if e in ("sync", "gpsimd"):
                self.op(e, lambda en: en.nop(), inc=(sems[e], 1))
            else:
                self.drain(e, inc=(sems[e], 1))
        waits = [(s, s.n) for s in sems.values()]
        if self.store_sem is not None:
            waits.append((self.store_sem, self.store_sem.n))
        for s_, v_ in (self.extra_barrier_waits() if self.extra_barrier_waits else []):
            waits.append((s_, v_))
        for e in ENGINES:
            self.op(e, lambda en: en.nop(), waits=waits)

    def finalize(self):
        block = self.es.enter_context(self.nc.Block())
        for eng in ENGINES:
            eng_ops = self.ops[eng]

            def body(e, eng_ops=eng_ops):
                for fn, ws, inc in eng_ops:
                    for (h, v) in ws:
                        e.wait_ge(h, v)
                    ins = fn(e)
                    if inc is not None:
                        ins.then_inc(inc[0], inc[1])

            getattr(block, eng)(body)


class Buf:
    __slots__ = ("arena", "off", "n")

    def __init__(self, arena, off, n):
        self.arena = arena
        self.off = off
        self.n = n

    def ap(self, c0=0, c1=None, p0=0, p1=128):
        c1 = self.n if c1 is None else c1
        assert 0 <= c0 <= c1 <= self.n, (c0, c1, self.n)
        return self.arena.t[p0:p1, self.off + c0:self.off + c1]

    def free(self):
        self.arena.free_(self.off, self.n)


class Arena:
    def __init__(self, nc, es, name, dtype, cols):
        self.t = es.enter_context(nc.sbuf_tensor(name, [128, cols], dtype))
        self.name = name
        self.cols = cols
        self.freelist = [[0, cols]]
        self.inuse = 0
        self.peak = 0

    def alloc(self, n, tag=""):
        n = (n + 7) & ~7
        for seg in self.freelist:
            if seg[1] >= n:
                off = seg[0]
                seg[0] += n
                seg[1] -= n
                if seg[1] == 0:
                    self.freelist.remove(seg)
                self.inuse += n
                self.peak = max(self.peak, self.inuse)
                return Buf(self, off, n)
        raise MemoryError(f"arena {self.name}: cannot alloc {n} cols for {tag} "
                          f"(inuse={self.inuse}/{self.cols}, free={self.freelist})")

    def free_(self, off, n):
        self.inuse -= n
        self.freelist.append([off, n])
        self.freelist.sort()
        merged = []
        for seg in self.freelist:
            if merged and merged[-1][0] + merged[-1][1] == seg[0]:
                merged[-1][1] += seg[1]
            else:
                merged.append(seg)
        self.freelist = merged


def _gammas():
    return 1.0 - np.exp2(-5.0 - np.arange(H, dtype=np.float64))


def host_consts():
    import jax
    import jax.numpy as jnp
    gam32 = (1.0 - np.exp2(-5.0 - np.arange(H, dtype=np.float32))).astype(np.float32)
    with jax.default_device(jax.devices("cpu")[0]):
        lng = jnp.log(jnp.asarray(gam32))
        n = jnp.arange(C, dtype=jnp.float32)
        E = np.asarray(jnp.exp(lng[:, None] * n[None, :]))
    ii = np.arange(C)
    tri = ii[None, :] >= ii[:, None]
    dmask = np.zeros((C, 16 * C), dtype=np.float32)
    for h in range(H):
        tile = np.where(tri, E[h][np.abs(ii[None, :] - ii[:, None])], np.float32(0.0))
        dmask[:, h * C:(h + 1) * C] = tile.astype(np.float32) * np.float32(0.125)
    ident = np.eye(C, dtype=np.float32)
    selc = np.zeros((16, 1345), dtype=np.float32)
    for h in range(16):
        selc[h, 64 * h:64 * h + 64] = 1.0
    selc[:, 1024:1088] = 1.0
    for k in range(16):
        b = k // 2
        if k % 2 == 0:
            selc[k, 1088 + np.arange(16) * 8 + b] = 1.0
        else:
            selc[k, 1216 + np.arange(16) * 8 + b] = 1.0
    selc[:, 1344] = gam32
    return dmask, ident, selc


GAMMAS_F = [float(1.0 - 2.0 ** (-5.0 - h)) for h in range(16)]


def pack_ws(w_io):
    K_, N_ = w_io.shape
    KC, OC = K_ // 128, N_ // 128
    return np.ascontiguousarray(
        np.asarray(w_io, dtype=np.float32).reshape(KC, 128, OC, 128)
        .transpose(2, 1, 0, 3).reshape(OC, 128, KC * 128))


VEC_NAMES = ["bq", "bk", "bv", "bg", "bo", "n1w", "n1b", "gnw", "gnb", "n2w", "n2b",
             "b_out", "binu", "bing", "s1", "s2", "sh1", "sh2", "s3"]
VEC_NCH = [FCD] * 12 + [KCI, KCI] + [FH] * 4 + [FCD]
VCOL = {}
_b = 0
for _n, _c in zip(VEC_NAMES, VEC_NCH):
    VCOL[_n] = _b
    _b += _c
NVCOL = _b


def build_program(upto="full", taps=()):
    taps = set(taps)
    debug = bool(taps) or upto != "full"
    nc = bass.Bass("TRN2", target_bir_lowering=False)
    es = contextlib.ExitStack()
    K = KB(nc, es)
    dt = nc.dram_tensor

    xT = dt("xT", [D, T], F32, kind="ExternalInput").ap()
    xTpre = dt("xTpre", [D, T], F32, kind="ExternalInput").ap()
    gate_in = dt("gate_in", [128], F32, kind="ExternalInput").ap()
    wq_p = dt("wq_p", [FCD, 128, D], F32, kind="ExternalInput").ap()
    wk_p = dt("wk_p", [FCD, 128, D], F32, kind="ExternalInput").ap()
    wv_p = dt("wv_p", [FCD, 128, D], F32, kind="ExternalInput").ap()
    wg_p = dt("wg_p", [FCD, 128, D], F32, kind="ExternalInput").ap()
    wo_p = dt("wo_p", [FCD, 128, D], F32, kind="ExternalInput").ap()
    win_u = dt("win_u", [KCI, 128, D], F32, kind="ExternalInput").ap()
    win_g = dt("win_g", [KCI, 128, D], F32, kind="ExternalInput").ap()
    wout_p = dt("wout_p", [FCD, 128, HPAD], F32, kind="ExternalInput").ap()
    w1_p = dt("w1_p", [FH, 128, D], F32, kind="ExternalInput").ap()
    w2_p = dt("w2_p", [FH, 128, D], F32, kind="ExternalInput").ap()
    h1_p = dt("h1_p", [FH, 128, DH], F32, kind="ExternalInput").ap()
    h2_p = dt("h2_p", [FH, 128, DH], F32, kind="ExternalInput").ap()
    w3_p = dt("w3_p", [FCD, 128, DH], F32, kind="ExternalInput").ap()
    vpack = dt("vpack", [128, NVCOL], F32, kind="ExternalInput").ap()
    brows = dt("brows", [2, D], F32, kind="ExternalInput").ap()
    mask_c = dt("mask_c", [C, 16 * C], F32, kind="ExternalInput").ap()
    selc_c = dt("selc_c", [16, 1345], F32, kind="ExternalInput").ap()
    ident_c = dt("ident_c", [C, C], F32, kind="ExternalInput").ap()

    out_d = dt("out", [D, T], F32, kind="ExternalOutput").ap()
    scratch_kind = "ExternalOutput"
    z_scr = dt("z_scr", [D, T], F32, kind=scratch_kind).ap()
    x1_scr = dt("x1_scr", [D, T], F32, kind=scratch_kind).ap()
    stat_scr = dt("stat_scr", [8, T], F32, kind=scratch_kind).ap()
    tap_ap = {}
    for name in taps:
        shp = {"xn": [D, T], "q": [D, T], "k": [D, T], "v": [D, T], "gate": [D, T],
               "p1": [D, T], "x2": [D, T], "pmat": [HPAD, T], "y2": [D, T],
               "h1a": [DH, T], "Af": [128, 512],
               "ret0": [128, 6 * 1024]}[name]
        tap_ap[name] = dt(f"tap_{name}", shp, BF16 if name not in ("Af",) else F32,
                          kind="ExternalOutput").ap()


    g_ = es.enter_context
    vps = g_(nc.sbuf_tensor("vps", [128, NVCOL], F32))
    cpool = g_(nc.sbuf_tensor("cpool", [128, 4], F32))
    gate_s = g_(nc.sbuf_tensor("gate_s", [128, 1], F32))
    ident_s = g_(nc.sbuf_tensor("ident_s", [C, C], BF16))
    ones_bf = g_(nc.sbuf_tensor("ones_bf", [128, 1], BF16))
    ones_f1 = g_(nc.sbuf_tensor("ones_f1", [1, 128], F32))
    rowbuf = g_(nc.sbuf_tensor("rowbuf", [1, T], F32))
    NPAN = 4
    PANW = 2048
    pan = [g_(nc.sbuf_tensor(f"pan{i}", [128, PANW], BF16)) for i in range(NPAN)]
    AB = Arena(nc, es, "AB", BF16, 66176)
    AF32 = Arena(nc, es, "AF32", F32, 12992)

    def vp(name, fc):
        c0 = VCOL[name] + fc
        return vps[:, c0:c0 + 1]

    s_vload = K.sem("vload")
    s_gload = K.sem("gload")
    s_pan = [K.sem(f"pan{i}") for i in range(NPAN)]
    s_panu = [K.sem(f"panu{i}") for i in range(NPAN)]
    NSTR = 2
    s_str = [K.sem(f"str{i}") for i in range(NSTR)]
    s_stru = [K.sem(f"stru{i}") for i in range(NSTR)]
    s_row = K.sem("row")
    s_mm = K.sem("mm")
    s_ev = K.sem("ev")
    s_sc = K.sem("sc")
    s_gp = K.sem("gp")
    s_jstr = [K.sem(f"jstr{i}") for i in range(2)]
    s_jstru = [K.sem(f"jstru{i}") for i in range(2)]
    s_sp = [K.sem(f"sp{i}") for i in range(2)]
    s_store = K.sem("store")
    K.store_sem = s_store
    K.extra_barrier_waits = lambda: [(s_sp[0], 16 * sp_rnd[0]), (s_sp[1], 16 * sp_rnd[1]),
                                     (s_row, s_row.n)]
    pan_rnd = [0] * NPAN
    str_rnd = [0, 0]
    jstr_rnd = [0, 0]
    sp_rnd = [0, 0]
    strb = [None, None]

    def panel_load(dram3, oc, slot, count=1, ncols=None, c0=0, c1=None):
        waits = [(s_panu[slot], s_panu[slot].n)] if pan_rnd[slot] > 0 else []
        c1 = ncols if c1 is None else c1
        w = c1 - c0
        assert count * w <= PANW
        if count == 1:
            K.op("gpsimd",
                 lambda e, d=dram3, o=oc, a=c0, b=c1, sl=slot: e.dma_start(
                     out=pan[sl][:, 0:b - a], in_=d[o, :, a:b]),
                 waits=waits, inc=(s_pan[slot], 16))
        else:
            K.op("gpsimd",
                 lambda e, d=dram3, o=oc, a=c0, b=c1, sl=slot, c=count: e.dma_start(
                     out=pan[sl][:, 0:c * (b - a)].rearrange("p (o n) -> p o n", n=b - a),
                     in_=d[o:o + c, :, a:b].rearrange("o p n -> p o n")),
                 waits=waits, inc=(s_pan[slot], 16))
        pan_rnd[slot] += 1
        return 16 * pan_rnd[slot]

    def stream_rows(dram2, fc, slot, cols=(0, T)):
        waits = [(s_stru[slot], s_stru[slot].n)] if str_rnd[slot] > 0 else []
        c0, c1 = cols
        bb = strb[slot]
        K.op("sync",
             lambda e, d=dram2, f=fc, bb=bb, a=c0, b=c1: e.dma_start(
                 out=bb.ap(0, b - a), in_=d[f * 128:(f + 1) * 128, a:b]),
             waits=waits, inc=(s_str[slot], 16))
        str_rnd[slot] += 1
        return 16 * str_rnd[slot]

    def spill(dram2, fc, src_ap, slot, cols=(0, T), producer_wait=()):
        c0, c1 = cols
        K.op("sync",
             lambda e, d=dram2, f=fc, s=src_ap, a=c0, b=c1: e.dma_start(
                 out=d[f * 128:(f + 1) * 128, a:b], in_=s),
             waits=producer_wait, inc=(s_sp[slot], 16))
        sp_rnd[slot] += 1
        return 16 * sp_rnd[slot]

    def mm(ps, lhs, rhs, start, stop, waits=(), inc=None):
        K.op("tensor",
             lambda e, p=ps, l=lhs, r=rhs, a=start, b=stop: e.matmul(p, l, r, start=a, stop=b),
             waits=waits, inc=inc)

    def row_load(ridx, cols=(0, T)):
        c0, c1 = cols
        K.op("sync", lambda e, r=ridx, a=c0, b=c1: e.dma_start(
            out=rowbuf[0:1, a:b], in_=stat_scr[r:r + 1, a:b]),
            waits=[(s_mm, s_mm.n)], inc=(s_row, 16))
        return s_row.n

    def tap_sync():
        for _e in ("tensor", "vector", "scalar"):
            K.op(_e, lambda en: en.nop(), waits=[(s_store, s_store.n)])

    def tap_gate():
        K.op("sync", lambda en: en.nop(), waits=[(s_store, s_store.n)])

    def bcast_row(ridx, ps, cols=(0, T)):
        rv = row_load(ridx, cols)
        c0, c1 = cols
        n = (c1 - c0) // 512
        for ts in range(n):
            mm(ps[:, ts * 512:(ts + 1) * 512], ones_f1[0:1, :],
               rowbuf[0:1, c0 + ts * 512:c0 + (ts + 1) * 512], True, True,
               waits=[(s_row, rv), (s_ev, s_ev.n), (s_sc, s_sc.n)] if ts == 0 else [],
               inc=(s_mm, 1) if ts == n - 1 else None)
        return s_mm.n

    K.op("sync", lambda e: e.dma_start(out=vps[:, :], in_=vpack[:, :]), inc=(s_vload, 16))
    K.op("sync", lambda e: e.dma_start(out=gate_s[:, 0], in_=gate_in[:]), inc=(s_vload, 16))
    K.op("gpsimd", lambda e: e.dma_start(out=ident_s[:, :], in_=ident_c[:, :]), inc=(s_gload, 16))
    K.op("vector", lambda e: e.memset(cpool[:, 0:1], EPS))
    K.op("vector", lambda e: e.memset(cpool[:, 1:2], -0.5))
    K.op("vector", lambda e: e.memset(cpool[:, 2:3], 1.0 / D))
    K.op("vector", lambda e: e.memset(cpool[:, 3:4], 1.0 / HD))
    K.op("vector", lambda e: e.memset(ones_bf[:, :], 1.0))
    K.op("vector", lambda e: e.memset(ones_f1[:, :], 1.0))
    K.drain("vector", inc=(s_ev, 1))
    for _e in ("tensor", "vector", "scalar"):
        K.op(_e, lambda en: en.nop(),
             waits=[(s_vload, s_vload.n), (s_gload, s_gload.n), (s_ev, s_ev.n)])
    K.barrier("init")

    gq_sb = AF32.alloc(FCD * C, "gq")
    gp_sb = AF32.alloc(D, "gp")
    gcpg_sb = AF32.alloc(8 * NCH, "gcpg")
    g16b = AF32.alloc(C, "g16")
    g16db = AF32.alloc(C, "g16d")
    g16sb = AF32.alloc(C, "g16s")
    pweb = AF32.alloc(136, "pwe")
    rhfb = [AF32.alloc(128, f"rhf{i}") for i in range(2)]
    selcb = AF32.alloc(1352, "selc")

    def _gsl(a, b, buf=None):
        return (buf or selcb).ap(a, b, p0=0, p1=16)

    K.op("sync", lambda e: e.dma_start(out=_gsl(0, 1345), in_=selc_c[:, :]),
         inc=(s_vload, 16))
    gld = s_vload.n
    K.op("gpsimd", lambda e: e.tensor_copy(_gsl(0, 1, g16b), _gsl(1344, 1345)),
         waits=[(s_vload, gld)])
    K.op("gpsimd", lambda e: e.memset(_gsl(127, 128, g16db), 1.0))
    K.op("gpsimd", lambda e: e.memset(_gsl(0, 8, pweb), 1.0))
    K.drain("gpsimd")
    for st in range(7):
        a = 1 << st
        K.op("gpsimd", lambda e, a=a: e.tensor_scalar_mul(
            _gsl(a, 2 * a, g16b), _gsl(0, a, g16b), _gsl(a - 1, a, g16b)))
        K.drain("gpsimd")
    for st in range(7):
        a = 1 << st
        K.op("gpsimd", lambda e, a=a: e.tensor_scalar_mul(
            _gsl(128 - 2 * a, 128 - a, g16db), _gsl(128 - a, 128, g16db),
            _gsl(a - 1, a, g16b)))
        K.drain("gpsimd")
    K.op("gpsimd", lambda e: e.tensor_scalar_mul(
        _gsl(0, 128, g16sb), _gsl(0, 128, g16b), 0.125))
    K.op("gpsimd", lambda e: e.tensor_scalar_mul(
        _gsl(8, 16, pweb), _gsl(0, 8, pweb), _gsl(127, 128, g16b)))
    K.op("gpsimd", lambda e: e.tensor_mul(
        _gsl(128, 129, pweb), _gsl(127, 128, g16b), _gsl(127, 128, g16b)))
    K.drain("gpsimd")
    K.op("gpsimd", lambda e: e.tensor_scalar_mul(
        _gsl(16, 32, pweb), _gsl(0, 16, pweb), _gsl(128, 129, pweb)))
    K.op("gpsimd", lambda e: e.tensor_mul(
        _gsl(129, 130, pweb), _gsl(128, 129, pweb), _gsl(128, 129, pweb)))
    K.drain("gpsimd")
    K.op("gpsimd", lambda e: e.tensor_scalar_mul(
        _gsl(32, 64, pweb), _gsl(0, 32, pweb), _gsl(129, 130, pweb)))
    K.op("gpsimd", lambda e: e.tensor_mul(
        _gsl(130, 131, pweb), _gsl(129, 130, pweb), _gsl(129, 130, pweb)))
    K.drain("gpsimd")
    K.op("gpsimd", lambda e: e.tensor_scalar_mul(
        _gsl(64, 128, pweb), _gsl(0, 64, pweb), _gsl(130, 131, pweb)))
    K.drain("gpsimd")
    K.op("gpsimd", lambda e: e.tensor_mul(
        _gsl(0, 128, rhfb[0]), _gsl(0, 128, pweb), _gsl(1088, 1216)))
    K.op("gpsimd", lambda e: e.tensor_mul(
        _gsl(0, 128, rhfb[1]), _gsl(0, 128, pweb), _gsl(1216, 1344)))
    K.drain("gpsimd", inc=(s_gp, 1))
    gv2 = s_gp.n
    with contextlib.ExitStack() as gh:
        ptA = gh.enter_context(nc.psum_tensor("ptA", [128, 512], F32))
        ptB = gh.enter_context(nc.psum_tensor("ptB", [128, 512], F32))
        for j_ in range(8):
            for bk in range(2):
                fc = (0 if bk == 0 else 4) + j_ // 2
                half = j_ % 2
                h = 2 * fc + half
                pt = ptA if bk == 0 else ptB
                mm(pt[64 * half:64 * half + 64,
                      (fc % 4) * 128:(fc % 4) * 128 + 128],
                   _gsl(64 * h, 64 * h + 64), _gsl(0, 128, g16sb), True, True,
                   waits=[(s_gp, gv2)] if (j_ == 0 and bk == 0) else [],
                   inc=(s_mm, 1) if j_ == 7 else None)
        gmq = s_mm.n
        K.op("vector", lambda e: e.tensor_copy(gq_sb.ap(0, 512), ptA[:, :]),
             waits=[(s_mm, gmq)])
        K.op("vector", lambda e: e.tensor_copy(gq_sb.ap(512, 1024), ptB[:, :]))
        K.drain("vector", inc=(s_ev, 1))
        gvq = s_ev.n
        for j_ in range(8):
            for bk in range(2):
                h = (0 if bk == 0 else 8) + j_
                pt = ptA if bk == 0 else ptB
                mm(pt[:, j_ * 64:(j_ + 1) * 64],
                   _gsl(0, 128, g16db), _gsl(64 * h, 64 * h + 64), True, True,
                   waits=[(s_ev, gvq)] if (j_ == 0 and bk == 0) else [],
                   inc=(s_mm, 1) if j_ == 7 else None)
        gmp = s_mm.n
        K.op("vector", lambda e: e.tensor_copy(gp_sb.ap(0, 512), ptA[:, :]),
             waits=[(s_mm, gmp)])
        K.op("vector", lambda e: e.tensor_copy(gp_sb.ap(512, 1024), ptB[:, :]))
        K.drain("vector", inc=(s_ev, 1))
        gvp = s_ev.n
        mm(ptA[0:64, 0:128], _gsl(1024, 1088), _gsl(0, 128, rhfb[0]), True, True,
           waits=[(s_ev, gvp)], inc=(s_mm, 1))
        mm(ptB[64:128, 0:128], _gsl(1024, 1088), _gsl(0, 128, rhfb[1]), True, True,
           inc=(s_mm, 1))
        gmc = s_mm.n
        K.op("vector", lambda e: e.tensor_copy(
            gcpg_sb.ap(0, 128, p0=0, p1=64), ptA[0:64, 0:128]),
            waits=[(s_mm, gmc)])
        K.op("vector", lambda e: e.tensor_copy(
            gcpg_sb.ap(0, 128, p0=64, p1=128), ptB[64:128, 0:128]))
        K.drain("vector", inc=(s_ev, 1))
    gen_done = s_ev.n
    K.barrier("gen")
    for _b in (g16b, g16db, g16sb, pweb, rhfb[0], rhfb[1], selcb):
        _b.free()

    def row_math(pstm, pstv, rows_out, double, tag, cols=(0, T), wait0=()):
        c0, c1 = cols
        w = c1 - c0
        rM = AF32.alloc(w, "rm_rM")
        rA = AF32.alloc(w, "rm_rA")
        rB = AF32.alloc(w, "rm_rB")
        rMa, rAa, rBa = rM.ap(p1=1), rA.ap(p1=1), rB.ap(p1=1)
        K.op("vector", lambda e: e.tensor_scalar_mul(rMa, pstm[0:1, 0:w], 1.0 / D),
             waits=list(wait0))
        K.op("vector", lambda e: e.tensor_scalar_mul(rAa, pstv[0:1, 0:w], 1.0 / D))
        K.drain("vector")
        K.op("vector", lambda e: e.tensor_mul(rBa, rMa, rMa))
        K.drain("vector")
        K.op("vector", lambda e: e.tensor_sub(rAa, rAa, rBa))
        K.drain("vector", inc=(s_ev, 1))
        ev0 = s_ev.n
        K.op("sync", lambda e, r=rows_out[0]: e.dma_start(
            out=stat_scr[r:r + 1, c0:c1], in_=rMa), waits=[(s_ev, ev0)], inc=(s_row, 16))
        rspill = s_row.n
        K.op("scalar", lambda e: e.activation(rBa, rAa, AF.Ln, bias=cpool[0:1, 0:1]),
             waits=[(s_ev, ev0)])
        K.drain("scalar")
        K.op("scalar", lambda e: e.mul(rBa, rBa, cpool[0:1, 1:2]))
        K.drain("scalar")
        K.op("scalar", lambda e: e.activation(rBa, rBa, AF.Exp))
        K.drain("scalar", inc=(s_sc, 1))
        sc1 = s_sc.n
        if double:
            K.op("vector", lambda e: e.tensor_mul(rMa, rAa, rBa),
                 waits=[(s_sc, sc1), (s_row, rspill)])
            K.drain("vector")
            K.op("vector", lambda e: e.tensor_mul(rMa, rMa, rBa))
            K.drain("vector", inc=(s_ev, 1))
            ev1 = s_ev.n
            K.op("scalar", lambda e: e.activation(rMa, rMa, AF.Ln, bias=cpool[0:1, 0:1]),
                 waits=[(s_ev, ev1)])
            K.drain("scalar")
            K.op("scalar", lambda e: e.mul(rMa, rMa, cpool[0:1, 1:2]))
            K.drain("scalar")
            K.op("scalar", lambda e: e.activation(rMa, rMa, AF.Exp))
            K.drain("scalar", inc=(s_sc, 1))
            sc2_ = s_sc.n
            K.op("vector", lambda e: e.tensor_mul(rAa, rBa, rMa), waits=[(s_sc, sc2_)])
            K.drain("vector", inc=(s_ev, 1))
            evf = s_ev.n
            rows = [(rows_out[1], rBa, [(s_ev, evf)]), (rows_out[2], rAa, [])]
        else:
            rows = [(rows_out[1], rBa, [(s_sc, sc1)])]
        for ridx, ap_, w0 in rows:
            K.op("sync", lambda e, r=ridx, a=ap_: e.dma_start(
                out=stat_scr[r:r + 1, c0:c1], in_=a), waits=w0, inc=(s_row, 16))
        K.op("sync", lambda e: e.nop(), waits=[(s_row, s_row.n)])
        for b in (rM, rA, rB):
            b.free()

    def ln_stats(src_dram, rows_out, double, tag):
        xb = AB.alloc(T, "st_xb")
        sq = AB.alloc(T, "st_sq")
        strb[0] = AF32.alloc(T, "st_s0")
        strb[1] = AF32.alloc(T, "st_s1")
        with contextlib.ExitStack() as ph:
            p_ = ph.enter_context
            pstm = p_(nc.psum_tensor(f"pstm_{tag}", [1, T], F32))
            pstv = p_(nc.psum_tensor(f"pstv_{tag}", [1, T], F32))
            for fc in range(FCD):
                sl = fc % NSTR
                wv = stream_rows(src_dram, fc, sl)
                bb = strb[sl]
                K.op("scalar", lambda e, bb=bb: e.copy(xb.ap(), bb.ap()),
                     waits=[(s_str[sl], wv), (s_mm, s_mm.n)], inc=(s_stru[sl], 1))
                K.drain("scalar", inc=(s_sc, 1))
                cpd = s_sc.n
                K.op("vector", lambda e: e.tensor_mul(sq.ap(), xb.ap(), xb.ap()),
                     waits=[(s_sc, cpd)])
                K.drain("vector", inc=(s_ev, 1))
                sqd = s_ev.n
                for ts in range(4):
                    mm(pstm[0:1, ts * 512:(ts + 1) * 512], ones_bf[:, 0:1],
                       xb.ap(ts * 512, (ts + 1) * 512), fc == 0, fc == FCD - 1,
                       waits=[(s_sc, cpd)] if ts == 0 else [])
                for ts in range(4):
                    mm(pstv[0:1, ts * 512:(ts + 1) * 512], ones_bf[:, 0:1],
                       sq.ap(ts * 512, (ts + 1) * 512), fc == 0, fc == FCD - 1,
                       waits=[(s_ev, sqd)] if ts == 0 else [],
                       inc=(s_mm, 1) if ts == 3 else None)
            mm_done = s_mm.n
            row_math(pstm, pstv, rows_out, double, tag, (0, T),
                     wait0=[(s_mm, mm_done)])
        for b in (xb, sq, strb[0], strb[1]):
            b.free()
        strb[0] = strb[1] = None
        K.barrier(f"st_{tag}")

    def ln_stats_job(src_dram, rows_out, double, tag):
        xb = AB.alloc(TH, f"stj_xb_{tag}")
        sq = AB.alloc(TH, f"stj_sq_{tag}")
        jb = [AF32.alloc(TH, f"stj_s0_{tag}"), AF32.alloc(TH, f"stj_s1_{tag}")]
        es_j = contextlib.ExitStack()
        pstm = es_j.enter_context(nc.psum_tensor(f"jpstm_{tag}", [1, TH], F32))
        pstv = es_j.enter_context(nc.psum_tensor(f"jpstv_{tag}", [1, TH], F32))
        yield
        for ch in range(2):
            c0, c1 = ch * TH, (ch + 1) * TH
            for fc in range(FCD):
                sl = fc % 2
                bb = jb[sl]
                waits = [(s_jstru[sl], s_jstru[sl].n)] if jstr_rnd[sl] > 0 else []
                K.op("sync",
                     lambda e, f=fc, bb=bb, a=c0, b=c1: e.dma_start(
                         out=bb.ap(0, TH), in_=src_dram[f * 128:(f + 1) * 128, a:b]),
                     waits=waits, inc=(s_jstr[sl], 16))
                jstr_rnd[sl] += 1
                wv = 16 * jstr_rnd[sl]
                K.op("scalar", lambda e, bb=bb: e.copy(xb.ap(), bb.ap()),
                     waits=[(s_jstr[sl], wv), (s_mm, s_mm.n)], inc=(s_jstru[sl], 1))
                K.drain("scalar", inc=(s_sc, 1))
                cpd = s_sc.n
                K.op("vector", lambda e: e.tensor_mul(sq.ap(), xb.ap(), xb.ap()),
                     waits=[(s_sc, cpd)])
                K.drain("vector", inc=(s_ev, 1))
                sqd = s_ev.n
                for ts in range(2):
                    mm(pstm[0:1, ts * 512:(ts + 1) * 512], ones_bf[:, 0:1],
                       xb.ap(ts * 512, (ts + 1) * 512), fc == 0, fc == FCD - 1,
                       waits=[(s_sc, cpd)] if ts == 0 else [])
                for ts in range(2):
                    mm(pstv[0:1, ts * 512:(ts + 1) * 512], ones_bf[:, 0:1],
                       sq.ap(ts * 512, (ts + 1) * 512), fc == 0, fc == FCD - 1,
                       waits=[(s_ev, sqd)] if ts == 0 else [],
                       inc=(s_mm, 1) if ts == 1 else None)
                yield
            row_math(pstm, pstv, rows_out, double, f"{tag}_{ch}", (c0, c1),
                     wait0=[(s_mm, s_mm.n)])
            yield
        es_j.close()
        for b in (xb, sq, jb[0], jb[1]):
            b.free()

    def apply_ln(src_dram, m_row, s_row_i, dst_fn, affine, tag, cols=(0, T)):
        c0, c1 = cols
        w = c1 - c0
        strb[0] = AF32.alloc(w, f"ap_s0_{tag}")
        strb[1] = AF32.alloc(w, f"ap_s1_{tag}")
        tf = AF32.alloc(w, f"ap_tf_{tag}")
        with contextlib.ExitStack() as ph:
            p_ = ph.enter_context
            psA = p_(nc.psum_tensor(f"psA_{tag}", [128, w], F32))
            psB = p_(nc.psum_tensor(f"psB_{tag}", [128, w], F32))
            bcast_row(m_row, psA, cols)
            bc_done = bcast_row(s_row_i, psB, cols)
            for fc in range(FCD):
                sl = fc % NSTR
                wv = stream_rows(src_dram, fc, sl, cols)
                bb = strb[sl]
                K.op("vector", lambda e, bb=bb, w=w: e.tensor_sub(
                    tf.ap(0, w), bb.ap(0, w), psA[:, 0:w]),
                    waits=[(s_str[sl], wv), (s_mm, bc_done),
                           (s_sc, s_sc.n) if affine else (s_ev, s_ev.n)],
                    inc=(s_stru[sl], 1))
                K.drain("vector")
                if affine:
                    K.op("vector", lambda e, w=w: e.tensor_mul(tf.ap(0, w), tf.ap(0, w),
                                                               psB[:, 0:w]),
                         inc=(s_ev, 1))
                    evd = s_ev.n
                    wn, bn = affine
                    K.op("scalar", lambda e, fc=fc, w=w, wn=wn, bn=bn: e.activation(
                        dst_fn(fc), tf.ap(0, w), AF.Identity, bias=vp(bn, fc), scale=vp(wn, fc)),
                        waits=[(s_ev, evd)], inc=(s_sc, 1))
                else:
                    K.op("vector", lambda e, fc=fc, w=w: e.tensor_mul(
                        dst_fn(fc), tf.ap(0, w), psB[:, 0:w]), inc=(s_ev, 1))
                    K.drain("vector")
            if affine:
                K.op("tensor", lambda e: e.nop(), waits=[(s_sc, s_sc.n)])
            else:
                K.op("tensor", lambda e: e.nop(), waits=[(s_ev, s_ev.n)])
        for b in (strb[0], strb[1], tf):
            b.free()
        strb[0] = strb[1] = None

    def chain_jobs(*gens):
        for g in gens:
            yield from g

    def apply_job(src_dram, m_row, s_row_i, dst_fn, affine, tag, cols=(0, T)):
        c0, c1 = cols
        w = c1 - c0
        jb = [AF32.alloc(w, f"jb0_{tag}"), AF32.alloc(w, f"jb1_{tag}")]
        tf = AF32.alloc(w, f"jtf_{tag}")
        es_j = contextlib.ExitStack()
        psA = es_j.enter_context(nc.psum_tensor(f"jpsA_{tag}", [128, w], F32))
        psB = es_j.enter_context(nc.psum_tensor(f"jpsB_{tag}", [128, w], F32))
        bcast_row(m_row, psA, cols)
        bc_done = bcast_row(s_row_i, psB, cols)
        yield
        for fc in range(FCD):
            sl = fc % 2
            bb = jb[sl]
            waits = [(s_jstru[sl], s_jstru[sl].n)] if jstr_rnd[sl] > 0 else []
            K.op("sync",
                 lambda e, f=fc, bb=bb, a=c0, b=c1: e.dma_start(
                     out=bb.ap(0, b - a), in_=src_dram[f * 128:(f + 1) * 128, a:b]),
                 waits=waits, inc=(s_jstr[sl], 16))
            jstr_rnd[sl] += 1
            wv = 16 * jstr_rnd[sl]
            K.op("vector", lambda e, bb=bb, w=w: e.tensor_sub(
                tf.ap(0, w), bb.ap(0, w), psA[:, 0:w]),
                waits=[(s_jstr[sl], wv), (s_mm, bc_done),
                       (s_sc, s_sc.n) if affine else (s_ev, s_ev.n)],
                inc=(s_jstru[sl], 1))
            K.drain("vector")
            if affine:
                K.op("vector", lambda e, w=w: e.tensor_mul(tf.ap(0, w), tf.ap(0, w),
                                                           psB[:, 0:w]),
                     inc=(s_ev, 1))
                evd = s_ev.n
                wn, bn = affine
                K.op("scalar", lambda e, fc=fc, w=w, wn=wn, bn=bn: e.activation(
                    dst_fn(fc), tf.ap(0, w), AF.Identity, bias=vp(bn, fc),
                    scale=vp(wn, fc)),
                    waits=[(s_ev, evd)], inc=(s_sc, 1))
            else:
                K.op("vector", lambda e, fc=fc, w=w: e.tensor_mul(
                    dst_fn(fc), tf.ap(0, w), psB[:, 0:w]), inc=(s_ev, 1))
                K.drain("vector")
            yield
        es_j.close()
        for b in (jb[0], jb[1], tf):
            b.free()

    def ws_layer(tag, panels, KC, OC, rhs_fn, evac, psG, cols=(0, T), pan_cols=None):
        t0, t1 = cols
        w = t1 - t0
        nts = w // 512
        pan_cols = pan_cols or KC * 128
        NG = len(psG)
        grp_free = [None] * NG
        if pan_cols <= PANW:
            G = PANW // pan_cols
            ngrp = (OC + G - 1) // G
            gw = []
            for gi in range(min(3, ngrp)):
                gw.append(panel_load(panels, gi * G, gi % NPAN,
                                     min(G, OC - gi * G), pan_cols))
            for oc in range(OC):
                g = oc % NG
                gi = oc // G
                slot = gi % NPAN
                off = (oc % G) * pan_cols
                waits = []
                if oc % G == 0:
                    waits.append((s_pan[slot], gw[gi]))
                if grp_free[g] is not None:
                    waits.append(grp_free[g])
                first = True
                grp_last = (oc % G == G - 1) or (oc == OC - 1)
                for kc in range(KC):
                    for ts in range(nts):
                        last = (kc == KC - 1 and ts == nts - 1)
                        inc = None
                        if last:
                            inc = (s_panu[slot], 1) if grp_last else (s_mm, 1)
                        mm(psG[g][:, ts * 512:(ts + 1) * 512],
                           pan[slot][:, off + kc * 128:off + (kc + 1) * 128],
                           rhs_fn(kc, ts),
                           kc == 0, kc == KC - 1,
                           waits=waits if first else [], inc=inc)
                        first = False
                mm_done = ((s_panu[slot], s_panu[slot].n) if grp_last
                           else (s_mm, s_mm.n))
                if oc % G == 0 and gi + 3 < ngrp:
                    gw.append(panel_load(panels, (gi + 3) * G, (gi + 3) % NPAN,
                                         min(G, OC - (gi + 3) * G), pan_cols))
                grp_free[g] = evac(oc, psG[g], mm_done)
                K.pump()
        else:
            KSPLIT = PANW // 128
            pw = []

            def load_pair(oc, pi):
                w0 = panel_load(panels, oc, 2 * pi, 1, None, 0, PANW)
                w1 = panel_load(panels, oc, 2 * pi + 1, 1, None, PANW, pan_cols)
                return (w0, w1)

            for oc in range(min(2, OC)):
                pw.append(load_pair(oc, oc % 2))
            for oc in range(OC):
                g = oc % NG
                pi = oc % 2
                waits = [(s_pan[2 * pi], pw[oc][0]), (s_pan[2 * pi + 1], pw[oc][1])]
                if grp_free[g] is not None:
                    waits.append(grp_free[g])
                first = True
                for kc in range(KC):
                    sl = 2 * pi + (1 if kc >= KSPLIT else 0)
                    off = (kc - KSPLIT) * 128 if kc >= KSPLIT else kc * 128
                    for ts in range(nts):
                        inc = None
                        if kc == KSPLIT - 1 and ts == nts - 1:
                            inc = (s_panu[2 * pi], 1)
                        elif kc == KC - 1 and ts == nts - 1:
                            inc = (s_panu[2 * pi + 1], 1)
                        mm(psG[g][:, ts * 512:(ts + 1) * 512],
                           pan[sl][:, off:off + 128],
                           rhs_fn(kc, ts),
                           kc == 0, kc == KC - 1,
                           waits=waits if first else [], inc=inc)
                        first = False
                mm_done = (s_panu[2 * pi + 1], s_panu[2 * pi + 1].n)
                if oc + 2 < OC:
                    pw.append(load_pair(oc + 2, oc % 2))
                grp_free[g] = evac(oc, psG[g], mm_done)
                K.pump()
        for g in range(NG):
            if grp_free[g] is not None:
                K.op("tensor", lambda e: e.nop(), waits=[grp_free[g]])

    A_fb = AF32.alloc(512, "A_f")
    A_bb = AB.alloc(512, "A_b")
    A_f, A_b = A_fb.ap(), A_bb.ap()

    def prefix_phase(pre_kv_job=None, state_job=None):
        ln_stats(xTpre, (RW_MZ, RW_SZ, RW_TMP), True, "pre")
        xnp = AB.alloc(FCD * T, "xnp")
        apply_ln(xTpre, RW_MZ, RW_TMP, lambda fc: xnp.ap(fc * T, (fc + 1) * T),
                 ("n1w", "n1b"), "preap")
        K.barrier("preap")
        kpre = AB.alloc(FCD * T, "kpre")
        vpre = AB.alloc(FCD * T, "vpre")
        with contextlib.ExitStack() as ph:
            p_ = ph.enter_context
            psG = [p_(nc.psum_tensor(f"psP{i}", [128, T], F32)) for i in range(2)]

            def pevac(dstb, bias):
                def evac(oc, ps, mmw):
                    K.op("scalar", lambda e, oc=oc, ps=ps, dstb=dstb, bias=bias:
                         e.activation(dstb.ap(oc * T, (oc + 1) * T), ps[:, :],
                                      AF.Identity, bias=vp(bias, oc)),
                         waits=[mmw], inc=(s_sc, 1))
                    return (s_sc, s_sc.n)
                return evac

            def rhs_xnp(kc, ts):
                return xnp.ap(kc * T + ts * 512, kc * T + (ts + 1) * 512)

            ws_layer("kpre", wk_p, FCD, FCD, rhs_xnp, pevac(kpre, "bk"), psG, (0, T))
            ws_layer("vpre", wv_p, FCD, FCD, rhs_xnp, pevac(vpre, "bv"), psG, (0, T))
        xnp.free()
        K.barrier("prekv")
        K.op("vector", lambda e: e.memset(A_f, 0.0))
        K.drain("vector", inc=(s_ev, 1))
        with contextlib.ExitStack() as ph:
            p_ = ph.enter_context
            psT_p = p_(nc.psum_tensor("psT_p", [128, 1024], BF16))
            st_p = [p_(nc.psum_tensor(f"st_p{i}", [128, 512], F32)) for i in range(2)]
            ktd = AB.alloc(D, "ktd_p")
            vtc = AB.alloc(D, "vtc_p")
            st_free = [0, 0]
            psT_free = 0
            K.job = state_job
            for c in range(NCH):
                cb = (c * C, (c + 1) * C)
                for fc in range(FCD):
                    K.op("tensor", lambda e, fc=fc, cb=cb: e.transpose(
                        psT_p[:, fc * 128:(fc + 1) * 128],
                        vpre.ap(fc * T + cb[0], fc * T + cb[1]), ident_s[:, :]),
                        waits=[(s_ev, psT_free)] if fc == 0 else [],
                        inc=(s_mm, 1) if fc == FCD - 1 else None)
                vtr = s_mm.n
                K.op("vector", lambda e: e.tensor_copy(vtc.ap(), psT_p[:, :]),
                     waits=[(s_mm, vtr)], inc=(s_ev, 1))
                vtc_ready = s_ev.n
                for fc in range(FCD):
                    K.op("tensor", lambda e, fc=fc, cb=cb: e.transpose(
                        psT_p[:, fc * 128:(fc + 1) * 128],
                        kpre.ap(fc * T + cb[0], fc * T + cb[1]), ident_s[:, :]),
                        waits=[(s_ev, vtc_ready)] if fc == 0 else [],
                        inc=(s_mm, 1) if fc == FCD - 1 else None)
                ktr = s_mm.n
                K.op("vector", lambda e: e.tensor_mul(ktd.ap(), psT_p[:, :], gp_sb.ap()),
                     waits=[(s_mm, ktr)], inc=(s_ev, 1))
                psT_free = s_ev.n
                ktd_ready = s_ev.n
                for w in range(4):
                    for i in range(4):
                        h = 4 * w + i
                        po = 64 * (h % 2)
                        col = 128 * w + 64 * ((h // 2) % 2)
                        mm(st_p[c % 2][po:po + 64, col:col + 64],
                           ktd.ap(h * 64, (h + 1) * 64),
                           vtc.ap(h * 64, (h + 1) * 64), True, True,
                           waits=[(s_ev, ktd_ready), (s_ev, st_free[c % 2])]
                           if (w == 0 and i == 0) else [],
                           inc=(s_mm, 1) if h == H - 1 else None)
                st_done = s_mm.n
                K.drain("vector")
                for b in range(8):
                    K.op("vector", lambda e, b=b, c=c: e.scalar_tensor_tensor(
                        A_fb.ap(64 * b, 64 * b + 64),
                        st_p[c % 2][:, 64 * b:64 * b + 64],
                        gcpg_sb.ap((15 - c) * 8 + b, (15 - c) * 8 + b + 1),
                        A_fb.ap(64 * b, 64 * b + 64),
                        ALU.mult, ALU.add),
                        waits=[(s_mm, st_done)] if b == 0 else [])
                K.drain("vector", inc=(s_ev, 1))
                st_free[c % 2] = s_ev.n
                K.pump()
            K.drain_job()
            K.op("tensor", lambda e: e.nop(), waits=[(s_ev, s_ev.n)])
            ktd.free()
            vtc.free()
        kpre.free()
        vpre.free()
        K.op("vector", lambda e: e.tensor_scalar_mul(A_f, A_f, gate_s[:, 0:1]))
        K.drain("vector")
        K.op("vector", lambda e: e.tensor_copy(A_b, A_f))
        K.drain("vector", inc=(s_ev, 1))
        K.barrier("prefix")

    ret_state = dict(sc_free=[0, 0], smb_free=[0, 0], rt_free=[0, 0], st_free=[0, 0],
                     aupd_blk=[0] * 4, gn_evac=0, psT_free=0, rtt_free=[0, 0],
                     pend_vec=[], pend_tail=None, gnn_done=0, gsc=0)

    def attn_half(hh, mask_sb, xn, qkvg_job=None, mask_w=0):
        t0, t1 = hh * TH, (hh + 1) * TH
        if hh == 0 and "xn" in tap_ap:
            tap_gate()
            for fc in range(FCD):
                K.op("sync", lambda e, fc=fc: e.dma_start(
                    out=tap_ap["xn"][fc * 128:(fc + 1) * 128, 0:TH],
                    in_=xn.ap(fc * TH, (fc + 1) * TH)), inc=(s_store, 16))
            tap_sync()
        qb = AB.alloc(FCD * TH, "qb")
        kb = AB.alloc(FCD * TH, "kb")
        vb = AB.alloc(FCD * TH, "vb")
        gb = AB.alloc(FCD * TH, "gb")
        with contextlib.ExitStack() as ph:
            p_ = ph.enter_context
            psG = [p_(nc.psum_tensor(f"psQ{i}_{hh}", [128, TH], F32)) for i in range(2)]

            def qevac(dstb, bias, func=AF.Identity):
                def evac(oc, ps, mmw):
                    K.op("scalar", lambda e, oc=oc, ps=ps: e.activation(
                        dstb.ap(oc * TH, (oc + 1) * TH), ps[:, :], func, bias=vp(bias, oc)),
                        waits=[mmw], inc=(s_sc, 1))
                    return (s_sc, s_sc.n)
                return evac

            def rhs_xn(kc, ts):
                return xn.ap(kc * TH + ts * 512, kc * TH + (ts + 1) * 512)

            K.job = qkvg_job
            ws_layer("q", wq_p, FCD, FCD, rhs_xn, qevac(qb, "bq"), psG, (t0, t1))
            ws_layer("k", wk_p, FCD, FCD, rhs_xn, qevac(kb, "bk"), psG, (t0, t1))
            ws_layer("v", wv_p, FCD, FCD, rhs_xn, qevac(vb, "bv"), psG, (t0, t1))
            ws_layer("g", wg_p, FCD, FCD, rhs_xn, qevac(gb, "bg", AF.Silu), psG, (t0, t1))
            K.drain_job()
        xn.free()
        K.barrier(f"qkvg{hh}")
        if hh == 0:
            for nm, bb in (("q", qb), ("k", kb), ("v", vb), ("gate", gb)):
                if nm in tap_ap:
                    tap_gate()
                    for fc in range(FCD):
                        K.op("sync", lambda e, fc=fc, bb=bb, nm=nm: e.dma_start(
                            out=tap_ap[nm][fc * 128:(fc + 1) * 128, 0:TH],
                            in_=bb.ap(fc * TH, (fc + 1) * TH)), inc=(s_store, 16))
                    tap_sync()
        S = ret_state
        NLC = NCH // NHALF
        with contextlib.ExitStack() as ph:
            p_ = ph.enter_context
            sc2 = [p_(nc.psum_tensor(f"sc2_{i}_{hh}", [128, 512], F32)) for i in range(2)]
            rt2 = [p_(nc.psum_tensor(f"rt2_{i}_{hh}", [128, 512], F32)) for i in range(2)]
            st2 = [p_(nc.psum_tensor(f"st2_{i}_{hh}", [128, 512], F32)) for i in range(2)]
            psT = p_(nc.psum_tensor(f"psT_{hh}", [128, 1024], BF16))
            psT2 = p_(nc.psum_tensor(f"psT2_{hh}", [128, 1024], BF16))
            qsb = [AB.alloc(D, f"qs{i}") for i in range(2)]
            ktd = AB.alloc(D, "ktd")
            vtc = AB.alloc(D, "vtc")
            smb = [AB.alloc(512, f"smb{i}") for i in range(2)]
            rttb = [AB.alloc(D, f"rtt{i}") for i in range(2)]
            gsq = AB.alloc(D, "gsq")
            gnn = AB.alloc(D, "gnn")
            gnf = AB.alloc(C, "gnf")
            gnst = AF32.alloc(64, "gnst")
            qs_ready = [0, 0]

            def emit_qsks(lc, cp):
                cb = (lc * C, (lc + 1) * C)
                for fc in range(FCD):
                    K.op("vector", lambda e, fc=fc, cb=cb, cp=cp: e.tensor_mul(
                        qsb[cp].ap(fc * C, (fc + 1) * C),
                        qb.ap(fc * TH + cb[0], fc * TH + cb[1]),
                        gq_sb.ap(fc * C, (fc + 1) * C)),
                        waits=[(s_mm, s_mm.n)] if fc == 0 else [])
                K.drain("vector", inc=(s_ev, 1))
                qs_ready[cp] = s_ev.n

            emit_qsks(0, 0)
            for lc in range(NLC):
                cp = lc % 2
                cb = (lc * C, (lc + 1) * C)
                sc_done = [0, 0]
                mask_done = [0, 0]
                rcs_done = [0, 0]

                def emit_score(w, i, extra=()):
                    h = 4 * w + i
                    fc, po = h // 2, 64 * (h % 2)
                    wb = w % 2
                    mm(sc2[wb][:, 128 * i:128 * i + 128],
                       kb.ap(fc * TH + cb[0], fc * TH + cb[1], p0=po, p1=po + 64),
                       qb.ap(fc * TH + cb[0], fc * TH + cb[1], p0=po, p1=po + 64),
                       True, True, waits=list(extra),
                       inc=(s_mm, 1) if i == 3 else None)

                def emit_mask(w):
                    wb = w % 2
                    mwx = ([(s_vload, mask_w)]
                           if (lc == 0 and w == 0 and mask_w) else [])
                    K.op("vector", lambda e, wb=wb, w=w: e.tensor_mul(
                        smb[wb].ap(), sc2[wb][:, :], mask_sb.ap(512 * w, 512 * (w + 1))),
                        waits=[(s_mm, sc_done[wb]), (s_mm, S["smb_free"][wb])] + mwx,
                        inc=(s_ev, 1))
                    mask_done[wb] = s_ev.n
                    S["sc_free"][wb] = s_ev.n

                def emit_rcs(w, i, extra=()):
                    h = 4 * w + i
                    fc, po = h // 2, 64 * (h % 2)
                    wb = w % 2
                    ro = 256 * (w // 2) + 64 * i
                    mm(rt2[wb][:, ro:ro + 64],
                       qsb[cp].ap(fc * C, (fc + 1) * C, p0=po, p1=po + 64),
                       A_bb.ap((h // 2) * 64, (h // 2) * 64 + 64, p0=po, p1=po + 64),
                       True, False, waits=list(extra))
                    mm(rt2[wb][:, ro:ro + 64],
                       smb[wb].ap(128 * i, 128 * i + 128),
                       vtc.ap(h * 64, (h + 1) * 64), False, True)
                    col = 128 * (w // 2) + 64 * ((h // 2) % 2)
                    mm(st2[wb][po:po + 64, col:col + 64],
                       ktd.ap(h * 64, (h + 1) * 64),
                       vtc.ap(h * 64, (h + 1) * 64), True, True,
                       inc=(s_mm, 1) if i == 3 else None)

                tw = [(s_ev, S["psT_free"]), (s_mm, s_mm.n)]
                for fc in range(FCD):
                    K.op("tensor", lambda e, fc=fc, cb=cb: e.transpose(
                        psT[:, fc * 128:(fc + 1) * 128],
                        vb.ap(fc * TH + cb[0], fc * TH + cb[1]), ident_s[:, :]),
                        waits=tw if fc == 0 else [],
                        inc=(s_mm, 1) if fc == FCD - 1 else None)
                    if fc < 4:
                        ex = ([(s_ev, S["sc_free"][0])] if fc == 0 else [])
                        emit_score(0, fc, ex)
                vtr = s_mm.n
                sc_done[0] = s_mm.n
                emit_mask(0)
                K.op("vector", lambda e: e.tensor_copy(vtc.ap(), psT[:, :]),
                     waits=[(s_mm, vtr)], inc=(s_ev, 1))
                vtc_ready = s_ev.n
                for fc in range(FCD):
                    K.op("tensor", lambda e, fc=fc, cb=cb: e.transpose(
                        psT[:, fc * 128:(fc + 1) * 128],
                        kb.ap(fc * TH + cb[0], fc * TH + cb[1]), ident_s[:, :]),
                        waits=[(s_ev, vtc_ready)] if fc == 0 else [],
                        inc=(s_mm, 1) if fc == FCD - 1 else None)
                ktr = s_mm.n
                K.op("vector", lambda e: e.tensor_mul(ktd.ap(), psT[:, :], gp_sb.ap()),
                     waits=[(s_mm, ktr)], inc=(s_ev, 1))
                S["psT_free"] = s_ev.n
                ktd_ready = s_ev.n
                for w in range(1, 5):
                    pw_ = w - 1
                    pwb = pw_ % 2
                    for i in range(4):
                        if w < 4:
                            wb = w % 2
                            ex = ([(s_ev, S["sc_free"][wb])] if i == 0 else [])
                            emit_score(w, i, ex)
                        ex = ([(s_ev, mask_done[pwb]), (s_ev, vtc_ready),
                               (s_ev, ktd_ready), (s_ev, qs_ready[cp]),
                               (s_sc, S["rt_free"][pwb]), (s_ev, S["st_free"][pwb]),
                               (s_ev, S["aupd_blk"][pw_])] if i == 0 else [])
                        emit_rcs(pw_, i, ex)
                    sc_done[w % 2] = s_mm.n if w < 4 else sc_done[w % 2]
                    rcs = s_mm.n
                    rcs_done[pwb] = rcs
                    S["smb_free"][pwb] = rcs
                    if w < 4:
                        emit_mask(w)
                    K.op("scalar", lambda e, pwb=pwb, pw_=pw_, cp=cp: e.copy(
                        rttb[cp].ap(256 * pw_, 256 * (pw_ + 1)),
                        rt2[pwb][:, 256 * (pw_ // 2):256 * (pw_ // 2) + 256]),
                        waits=[(s_mm, rcs)] + ([(s_ev, S["rtt_free"][cp])] if pw_ == 0 else []),
                        inc=(s_sc, 1))
                    S["rt_free"][pwb] = s_sc.n
                    K.drain("vector")
                    for bi in range(2):
                        b = 2 * pw_ + bi
                        K.op("vector", lambda e, pwb=pwb, pw_=pw_, b=b, bi=bi:
                             e.scalar_tensor_tensor(
                                 A_fb.ap(64 * b, 64 * b + 64),
                                 A_fb.ap(64 * b, 64 * b + 64),
                                 gcpg_sb.ap(8 + b, 9 + b),
                                 st2[pwb][:, 128 * (pw_ // 2) + 64 * bi:
                                          128 * (pw_ // 2) + 64 * bi + 64],
                                 ALU.mult, ALU.add),
                             waits=[(s_mm, rcs)] if bi == 0 else [])
                    K.drain("vector")
                    K.op("vector", lambda e, pw_=pw_: e.tensor_copy(
                        A_bb.ap(128 * pw_, 128 * (pw_ + 1)),
                        A_fb.ap(128 * pw_, 128 * (pw_ + 1))),
                        inc=(s_ev, 1))
                    S["aupd_blk"][pw_] = s_ev.n
                    S["st_free"][pwb] = s_ev.n
                    if S["pend_vec"]:
                        S["pend_vec"].pop(0)()
                    if w == 3 and lc + 1 < NLC:
                        emit_qsks(lc + 1, 1 - cp)
                while S["pend_vec"]:
                    S["pend_vec"].pop(0)()
                if S["pend_tail"] is not None:
                    S["pend_tail"]()
                    S["pend_tail"] = None

                def gn_s1(cp=cp):
                    K.op("vector", lambda e: e.tensor_mul(gsq.ap(), rttb[cp].ap(),
                                                          rttb[cp].ap()),
                         waits=[(s_sc, s_sc.n), (s_mm, s_mm.n)])
                    K.drain("vector")

                def gn_s2(cp=cp):
                    K.op("vector", lambda e: e.tensor_reduce(
                        gnst.ap(0, 16), rttb[cp].ap().rearrange("p (h d) -> p h d", d=64),
                        AX.X, ALU.add))
                    K.drain("vector")

                def gn_s3(cp=cp):
                    K.op("vector", lambda e: e.tensor_reduce(
                        gnst.ap(16, 32), gsq.ap().rearrange("p (h d) -> p h d", d=64),
                        AX.X, ALU.add))
                    K.drain("vector")

                def gn_s4(cp=cp):
                    K.op("vector", lambda e: e.tensor_scalar_mul(
                        gnst.ap(32, 48), gnst.ap(0, 16), cpool[:, 3:4]))
                    K.drain("vector")
                    K.op("vector", lambda e: e.tensor_mul(gnst.ap(48, 64), gnst.ap(32, 48),
                                                          gnst.ap(32, 48)))
                    K.drain("vector")
                    K.op("vector", lambda e: e.scalar_tensor_tensor(
                        gnst.ap(0, 16), gnst.ap(16, 32), cpool[:, 3:4], gnst.ap(48, 64),
                        ALU.mult, ALU.subtract))
                    K.drain("vector", inc=(s_ev, 1))
                    gve = s_ev.n
                    K.op("scalar", lambda e: e.activation(gnst.ap(16, 32), gnst.ap(0, 16),
                                                          AF.Ln, bias=cpool[:, 0:1]),
                         waits=[(s_ev, gve)])
                    K.drain("scalar")
                    K.op("scalar", lambda e: e.mul(gnst.ap(16, 32), gnst.ap(16, 32),
                                                   cpool[:, 1:2]))
                    K.drain("scalar")
                    K.op("scalar", lambda e: e.activation(gnst.ap(16, 32), gnst.ap(16, 32),
                                                          AF.Exp))
                    K.drain("scalar", inc=(s_sc, 1))
                    S["gsc"] = s_sc.n

                def gn_s5(cp=cp):
                    gsc = S["gsc"]
                    for h in range(H):
                        K.op("vector", lambda e, h=h: e.tensor_scalar(
                            gnn.ap(h * 64, (h + 1) * 64), rttb[cp].ap(h * 64, (h + 1) * 64),
                            gnst.ap(32 + h, 33 + h), gnst.ap(16 + h, 17 + h),
                            ALU.subtract, ALU.mult),
                            waits=[(s_sc, gsc)] if h == 0 else [])
                    K.drain("vector", inc=(s_ev, 1))
                    S["rtt_free"][cp] = s_ev.n
                    S["gnn_done"] = s_ev.n

                def gn_vec():
                    for f in (gn_s1, gn_s2, gn_s3, gn_s4, gn_s5):
                        f()

                def gn_tail(cb=cb):
                    gnn_done = S["gnn_done"]
                    for fc in range(FCD):
                        K.op("tensor", lambda e, fc=fc: e.transpose(
                            psT2[:, fc * 128:(fc + 1) * 128],
                            gnn.ap(fc * 128, (fc + 1) * 128), ident_s[:, :]),
                            waits=[(s_ev, gnn_done), (s_sc, S["gn_evac"])] if fc == 0 else [],
                            inc=(s_mm, 1) if fc == FCD - 1 else None)
                    gtr = s_mm.n
                    for fc in range(FCD):
                        K.op("scalar", lambda e, fc=fc: e.activation(
                            gnf.ap(), psT2[:, fc * 128:(fc + 1) * 128], AF.Identity,
                            bias=vp("gnb", fc), scale=vp("gnw", fc)),
                            waits=[(s_mm, gtr), (s_ev, s_ev.n)] if fc == 0
                            else [(s_ev, s_ev.n)],
                            inc=(s_sc, 1))
                        gfa = s_sc.n
                        K.op("vector", lambda e, fc=fc, cb=cb: e.tensor_mul(
                            gb.ap(fc * TH + cb[0], fc * TH + cb[1]),
                            gb.ap(fc * TH + cb[0], fc * TH + cb[1]), gnf.ap()),
                            waits=[(s_sc, gfa)], inc=(s_ev, 1))
                    S["gn_evac"] = s_sc.n

                S["pend_vec"] = [gn_s1, gn_s2, gn_s3, gn_s4, gn_s5]
                S["pend_tail"] = gn_tail
            while S["pend_vec"]:
                S["pend_vec"].pop(0)()
            if S["pend_tail"] is not None:
                S["pend_tail"]()
                S["pend_tail"] = None
            K.op("tensor", lambda e: e.nop(), waits=[(s_sc, s_sc.n), (s_ev, s_ev.n),
                                                     (s_gp, s_gp.n)])
            for b in (qsb[0], qsb[1], ktd, vtc, smb[0], smb[1],
                      rttb[0], rttb[1], gsq, gnn, gnf, gnst):
                b.free()
        qb.free()
        kb.free()
        vb.free()
        K.barrier(f"ret{hh}")
        if hh == 0 and "p1" in tap_ap:
            tap_gate()
            for fc in range(FCD):
                K.op("sync", lambda e, fc=fc: e.dma_start(
                    out=tap_ap["p1"][fc * 128:(fc + 1) * 128, 0:TH],
                    in_=gb.ap(fc * TH, (fc + 1) * TH)), inc=(s_store, 16))
            tap_sync()
        m1sb = AF32.alloc(TH, "m1sb")
        r1sb = AF32.alloc(TH, "r1sb")
        yws = AF32.alloc(TH, "yws")
        zws = [AF32.alloc(TH, f"zw{i}") for i in range(2)]
        zbf = AB.alloc(TH, "zbf")
        zsq = AB.alloc(TH, "zsq")
        strb[0] = AF32.alloc(TH, "wo_s0")
        strb[1] = AF32.alloc(TH, "wo_s1")
        with contextlib.ExitStack() as ph:
            p_ = ph.enter_context
            with contextlib.ExitStack() as ph2:
                psb2 = ph2.enter_context(nc.psum_tensor(f"psb2_{hh}", [128, TH], F32))
                bcast_row(RW_M1, psb2, (t0, t1))
                K.op("scalar", lambda e: e.copy(m1sb.ap(), psb2[:, :]),
                     waits=[(s_mm, s_mm.n)], inc=(s_sc, 1))
                bcast_row(RW_R1, psb2, (t0, t1))
                K.op("scalar", lambda e: e.copy(r1sb.ap(), psb2[:, :]),
                     waits=[(s_mm, s_mm.n)], inc=(s_sc, 1))
                K.op("tensor", lambda e: e.nop(), waits=[(s_sc, s_sc.n)])
            pstz = p_(nc.psum_tensor(f"pstz_{hh}", [1, TH], F32))
            pstv2 = p_(nc.psum_tensor(f"pstv2_{hh}", [1, TH], F32))
            psG = [p_(nc.psum_tensor(f"psGo{i}_{hh}", [128, TH], F32)) for i in range(2)]

            def oevac(oc, ps, mmw):
                sl = oc % 2
                wv = stream_rows(xT, oc, sl, (t0, t1))
                bb = strb[sl]
                spw = [(s_sp[sl], 16 * sp_rnd[sl])] if sp_rnd[sl] > 0 else []
                K.op("vector", lambda e, bb=bb: e.tensor_sub(
                    yws.ap(), bb.ap(), m1sb.ap()),
                    waits=[(s_str[sl], wv), (s_sc, s_sc.n), (s_ev, s_ev.n)],
                    inc=(s_stru[sl], 1))
                K.drain("vector")
                K.op("vector", lambda e: e.tensor_mul(yws.ap(), yws.ap(), r1sb.ap()))
                K.drain("vector")
                K.op("vector", lambda e, oc=oc, ps=ps, sl=sl: e.scalar_tensor_tensor(
                    zws[sl].ap(), ps[:, :], vp("bo", oc), yws.ap(), ALU.add, ALU.add),
                    waits=[mmw] + spw, inc=(s_ev, 1))
                evd = s_ev.n
                spill(z_scr, oc, zws[sl].ap(), sl, (t0, t1), [(s_ev, evd)])
                K.op("scalar", lambda e, sl=sl: e.copy(zbf.ap(), zws[sl].ap()),
                     waits=[(s_ev, evd), (s_mm, s_mm.n)])
                K.drain("scalar", inc=(s_sc, 1))
                cpd = s_sc.n
                K.op("vector", lambda e: e.tensor_mul(zsq.ap(), zbf.ap(), zbf.ap()),
                     waits=[(s_sc, cpd)])
                K.drain("vector", inc=(s_ev, 1))
                sqd = s_ev.n
                for ts in range(2):
                    mm(pstz[0:1, ts * 512:(ts + 1) * 512], ones_bf[:, 0:1],
                       zbf.ap(ts * 512, (ts + 1) * 512), oc == 0, oc == FCD - 1,
                       waits=[(s_sc, cpd)] if ts == 0 else [])
                for ts in range(2):
                    mm(pstv2[0:1, ts * 512:(ts + 1) * 512], ones_bf[:, 0:1],
                       zsq.ap(ts * 512, (ts + 1) * 512), oc == 0, oc == FCD - 1,
                       waits=[(s_ev, sqd)] if ts == 0 else [],
                       inc=(s_mm, 1) if ts == 1 else None)
                return (s_ev, evd)

            ws_layer("wo", wo_p, FCD, FCD,
                     lambda kc, ts: gb.ap(kc * TH + ts * 512, kc * TH + (ts + 1) * 512),
                     oevac, psG, (t0, t1))
            for b in (m1sb, r1sb, yws):
                b.free()
            row_math(pstz, pstv2, (RW_MZ, RW_SZ), False, f"zst{hh}", (t0, t1),
                     wait0=[(s_mm, s_mm.n)])
            K.op("sync", lambda e: e.nop(),
                 waits=[(s_sp[0], 16 * sp_rnd[0]), (s_sp[1], 16 * sp_rnd[1])])
        for b in (zws[0], zws[1], zbf, zsq, strb[0], strb[1]):
            b.free()
        strb[0] = strb[1] = None
        gb.free()
        K.barrier(f"wo{hh}")

    def phaseC_half(hh, x2, ug_job=None):
        t0, t1 = hh * TH, (hh + 1) * TH
        if hh == 0 and "x2" in tap_ap:
            tap_gate()
            for fc in range(FCD):
                K.op("sync", lambda e, fc=fc: e.dma_start(
                    out=tap_ap["x2"][fc * 128:(fc + 1) * 128, 0:TH],
                    in_=x2.ap(fc * TH, (fc + 1) * TH)), inc=(s_store, 16))
            tap_sync()
        pmat = AB.alloc(KCI * TH, "pmat")
        usil = [AB.alloc(TH, f"usil{i}") for i in range(2)]
        with contextlib.ExitStack() as ph:
            p_ = ph.enter_context
            psG = [p_(nc.psum_tensor(f"psGc{i}_{hh}", [128, TH], F32)) for i in range(2)]
            grp_free = [None] * 2
            G = PANW // D
            ngrp = (KCI + G - 1) // G
            gwj = {0: [], 1: []}
            K.job = ug_job
            for j, panels in ((0, win_u), (1, win_g)):
                for gi in range(min(2, ngrp)):
                    gwj[j].append(panel_load(panels, gi * G, j + 2 * gi, min(G, KCI - gi * G), D))
            for oc in range(KCI):
                for j, panels in ((0, win_u), (1, win_g)):
                    gi = oc // G
                    slot = j + 2 * (gi % 2)
                    off = (oc % G) * D
                    pg = j
                    waits = []
                    if oc % G == 0:
                        waits.append((s_pan[slot], gwj[j][gi]))
                    if grp_free[pg] is not None:
                        waits.append(grp_free[pg])
                    first = True
                    grp_last = (oc % G == G - 1) or (oc == KCI - 1)
                    for kc in range(FCD):
                        for ts in range(TH // 512):
                            last = (kc == FCD - 1 and ts == TH // 512 - 1)
                            inc = None
                            if last:
                                inc = (s_panu[slot], 1) if grp_last else (s_mm, 1)
                            mm(psG[pg][:, ts * 512:(ts + 1) * 512],
                               pan[slot][:, off + kc * 128:off + (kc + 1) * 128],
                               x2.ap(kc * TH + ts * 512, kc * TH + (ts + 1) * 512),
                               kc == 0, kc == FCD - 1,
                               waits=waits if first else [],
                               inc=inc)
                            first = False
                    mm_done = ((s_panu[slot], s_panu[slot].n) if grp_last
                               else (s_mm, s_mm.n))
                    if grp_last and gi + 2 < ngrp:
                        gwj[j].append(panel_load(panels, (gi + 2) * G, j + 2 * (gi % 2),
                                                 min(G, KCI - (gi + 2) * G), D))
                    if j == 0:
                        K.op("scalar", lambda e, oc=oc, ps=psG[pg]: e.activation(
                            usil[oc % 2].ap(), ps[:, :], AF.Silu, bias=vp("binu", oc)),
                            waits=[mm_done, (s_ev, s_ev.n)], inc=(s_sc, 1))
                        grp_free[pg] = (s_sc, s_sc.n)
                    else:
                        K.op("vector", lambda e, oc=oc, ps=psG[pg]: e.scalar_tensor_tensor(
                            pmat.ap(oc * TH, (oc + 1) * TH), ps[:, :], vp("bing", oc),
                            usil[oc % 2].ap(), ALU.add, ALU.mult),
                            waits=[mm_done, (s_sc, s_sc.n)], inc=(s_ev, 1))
                        grp_free[pg] = (s_ev, s_ev.n)
                    K.pump()
            K.drain_job()
            K.op("tensor", lambda e: e.nop(), waits=[gf for gf in grp_free if gf is not None])
        x2.free()
        K.barrier(f"pmat{hh}")
        if hh == 0 and "pmat" in tap_ap:
            tap_gate()
            for kc in range(KCI):
                K.op("sync", lambda e, kc=kc: e.dma_start(
                    out=tap_ap["pmat"][kc * 128:(kc + 1) * 128, 0:TH],
                    in_=pmat.ap(kc * TH, (kc + 1) * TH)), inc=(s_store, 16))
            tap_sync()
        tws = [AF32.alloc(TH, f"tw{i}") for i in range(2)]
        x1bf = AB.alloc(TH, "x1bf")
        x1sq = AB.alloc(TH, "x1sq")
        strb[0] = AF32.alloc(TH, "wt_s0")
        strb[1] = AF32.alloc(TH, "wt_s1")
        with contextlib.ExitStack() as ph:
            p_ = ph.enter_context
            pstx = p_(nc.psum_tensor(f"pstx_{hh}", [1, TH], F32))
            pstxv = p_(nc.psum_tensor(f"pstxv_{hh}", [1, TH], F32))
            psGW = [p_(nc.psum_tensor(f"psGw{i}_{hh}", [128, TH], F32)) for i in range(2)]

            def wevac(oc, ps, mmw):
                wz = stream_rows(z_scr, oc, 0, (t0, t1))
                wx = stream_rows(xT, oc, 1, (t0, t1))
                b0, b1 = strb[0], strb[1]
                sl = oc % 2
                spw = [(s_sp[sl], 16 * sp_rnd[sl])] if sp_rnd[sl] > 0 else []
                K.op("vector", lambda e, oc=oc, ps=ps, sl=sl, b0=b0: e.scalar_tensor_tensor(
                    tws[sl].ap(), ps[:, :], vp("b_out", oc), b0.ap(),
                    ALU.add, ALU.add),
                    waits=[mmw, (s_str[0], wz)] + spw, inc=(s_stru[0], 1))
                K.drain("vector")
                K.op("vector", lambda e, sl=sl, b1=b1: e.tensor_add(
                    tws[sl].ap(), tws[sl].ap(), b1.ap()),
                    waits=[(s_str[1], wx)], inc=(s_stru[1], 1))
                K.drain("vector", inc=(s_ev, 1))
                evd = s_ev.n
                spill(x1_scr, oc, tws[sl].ap(), sl, (t0, t1), [(s_ev, evd)])
                K.op("scalar", lambda e, sl=sl: e.copy(x1bf.ap(), tws[sl].ap()),
                     waits=[(s_ev, evd), (s_mm, s_mm.n)])
                K.drain("scalar", inc=(s_sc, 1))
                cpd = s_sc.n
                K.op("vector", lambda e: e.tensor_mul(x1sq.ap(), x1bf.ap(), x1bf.ap()),
                     waits=[(s_sc, cpd)])
                K.drain("vector", inc=(s_ev, 1))
                sqd = s_ev.n
                for ts in range(2):
                    mm(pstx[0:1, ts * 512:(ts + 1) * 512], ones_bf[:, 0:1],
                       x1bf.ap(ts * 512, (ts + 1) * 512), oc == 0, oc == FCD - 1,
                       waits=[(s_sc, cpd)] if ts == 0 else [])
                for ts in range(2):
                    mm(pstxv[0:1, ts * 512:(ts + 1) * 512], ones_bf[:, 0:1],
                       x1sq.ap(ts * 512, (ts + 1) * 512), oc == 0, oc == FCD - 1,
                       waits=[(s_ev, sqd)] if ts == 0 else [],
                       inc=(s_mm, 1) if ts == 1 else None)
                return (s_ev, evd)

            ws_layer("wout", wout_p, KCI, FCD,
                     lambda kc, ts: pmat.ap(kc * TH + ts * 512, kc * TH + (ts + 1) * 512),
                     wevac, psGW, (t0, t1), pan_cols=HPAD)
            row_math(pstx, pstxv, (RW_MY, RW_RY), False, f"x1st{hh}", (t0, t1),
                     wait0=[(s_mm, s_mm.n)])
            K.op("sync", lambda e: e.nop(),
                 waits=[(s_sp[0], 16 * sp_rnd[0]), (s_sp[1], 16 * sp_rnd[1])])
        for b in (pmat, usil[0], usil[1], tws[0], tws[1], x1bf, x1sq,
                  strb[0], strb[1]):
            b.free()
        strb[0] = strb[1] = None
        K.barrier(f"phC{hh}")

    def phaseD_half(hh, y2h, w3_job=None):
        t0, t1 = hh * TH, (hh + 1) * TH
        h1a = AB.alloc(FH * TH, "h1a")
        if hh == 0 and "y2" in tap_ap:
            tap_gate()
            for fc in range(FCD):
                K.op("sync", lambda e, fc=fc: e.dma_start(
                    out=tap_ap["y2"][fc * 128:(fc + 1) * 128, 0:TH],
                    in_=y2h.ap(fc * TH, (fc + 1) * TH)), inc=(s_store, 16))
            tap_sync()
        usil = [AB.alloc(TH, f"usD{i}") for i in range(2)]
        with contextlib.ExitStack() as ph:
            p_ = ph.enter_context
            psH = [p_(nc.psum_tensor(f"psH{i}_{hh}", [128, TH], F32)) for i in range(2)]
            grp_free = [None] * 2
            G = PANW // D
            ngrp = (FH + G - 1) // G
            gwj = {0: [], 1: []}
            for j, panels in ((0, w1_p), (1, w2_p)):
                for gi in range(min(2, ngrp)):
                    gwj[j].append(panel_load(panels, gi * G, j + 2 * gi, G, D))
            for oc in range(FH):
                for j, panels in ((0, w1_p), (1, w2_p)):
                    g = j
                    gi = oc // G
                    slot = j + 2 * (gi % 2)
                    off = (oc % G) * D
                    waits = []
                    if oc % G == 0:
                        waits.append((s_pan[slot], gwj[j][gi]))
                    if grp_free[g] is not None:
                        waits.append(grp_free[g])
                    first = True
                    grp_last = (oc % G == G - 1) or (oc == FH - 1)
                    for kc in range(FCD):
                        for ts in range(2):
                            last = (kc == FCD - 1 and ts == 1)
                            inc = None
                            if last:
                                inc = (s_panu[slot], 1) if grp_last else (s_mm, 1)
                            mm(psH[g][:, ts * 512:(ts + 1) * 512],
                               pan[slot][:, off + kc * 128:off + (kc + 1) * 128],
                               y2h.ap(kc * TH + ts * 512, kc * TH + (ts + 1) * 512),
                               kc == 0, kc == FCD - 1,
                               waits=waits if first else [],
                               inc=inc)
                            first = False
                    mm_done = ((s_panu[slot], s_panu[slot].n) if grp_last
                               else (s_mm, s_mm.n))
                    if grp_last and gi + 2 < ngrp:
                        gwj[j].append(panel_load(panels, (gi + 2) * G, j + 2 * (gi % 2),
                                                 G, D))
                    if j == 0:
                        K.op("scalar", lambda e, oc=oc, g=g: e.activation(
                            usil[oc % 2].ap(), psH[g][:, :], AF.Silu, scale=vp("s1", oc)),
                            waits=[mm_done, (s_ev, s_ev.n)], inc=(s_sc, 1))
                        grp_free[g] = (s_sc, s_sc.n)
                    else:
                        K.op("vector", lambda e, oc=oc, g=g: e.scalar_tensor_tensor(
                            h1a.ap(oc * TH, (oc + 1) * TH), psH[g][:, :], vp("s2", oc),
                            usil[oc % 2].ap(), ALU.mult, ALU.mult),
                            waits=[mm_done, (s_sc, s_sc.n)], inc=(s_ev, 1))
                        grp_free[g] = (s_ev, s_ev.n)
                    K.pump()
            K.drain_job()
            for gf in grp_free:
                K.op("tensor", lambda e: e.nop(), waits=[gf])
            K.barrier(f"h1a_{hh}")
            if hh == 0 and "h1a" in tap_ap:
                tap_gate()
                for kc in range(FH):
                    K.op("sync", lambda e, kc=kc: e.dma_start(
                        out=tap_ap["h1a"][kc * 128:(kc + 1) * 128, 0:TH],
                        in_=h1a.ap(kc * TH, (kc + 1) * TH)), inc=(s_store, 16))
                tap_sync()
            y2h.free()
            usil[0].free()
            usil[1].free()
            h2a = AB.alloc(FH * TH, "h2a")
            outst = [AF32.alloc(TH, f"outst{i}") for i in range(2)]
            strb[0] = AF32.alloc(TH, "d_s0")
            strb[1] = AF32.alloc(TH, "d_s1")

            def gelu_evac(dstb, sname):
                def evac(oc, ps, mmw):
                    K.op("scalar", lambda e, oc=oc, ps=ps, dstb=dstb, sname=sname:
                         e.activation(dstb.ap(oc * TH, (oc + 1) * TH), ps[:, :],
                                      AF.Gelu, scale=vp(sname, oc)),
                         waits=[mmw], inc=(s_sc, 1))
                    return (s_sc, s_sc.n)
                return evac

            def out_evac(oc, ps, mmw):
                sl = oc % 2
                wx = stream_rows(x1_scr, oc, sl, (t0, t1))
                bb = strb[sl]
                spw = [(s_sp[sl], 16 * sp_rnd[sl])] if sp_rnd[sl] > 0 else []
                K.op("vector", lambda e, oc=oc, ps=ps, sl=sl, bb=bb: e.scalar_tensor_tensor(
                    outst[sl].ap(), ps[:, :], vp("s3", oc),
                    bb.ap(), ALU.mult, ALU.add),
                    waits=[mmw, (s_str[sl], wx)] + spw,
                    inc=(s_stru[sl], 1))
                K.drain("vector", inc=(s_ev, 1))
                evd = s_ev.n
                K.op("sync", lambda e, oc=oc, sl=sl, a=t0, b=t1: e.dma_start(
                    out=out_d[oc * 128:(oc + 1) * 128, a:b], in_=outst[sl].ap()),
                    waits=[(s_ev, evd)], inc=(s_sp[sl], 16))
                sp_rnd[sl] += 1
                return (s_ev, evd)

            psH2 = psH
            ws_layer("h1", h1_p, FH, FH,
                     lambda kc, ts: h1a.ap(kc * TH + ts * 512, kc * TH + (ts + 1) * 512),
                     gelu_evac(h2a, "sh1"), psH2, (t0, t1), pan_cols=DH)
            K.barrier(f"h2a_{hh}")
            ws_layer("h2", h2_p, FH, FH,
                     lambda kc, ts: h2a.ap(kc * TH + ts * 512, kc * TH + (ts + 1) * 512),
                     gelu_evac(h1a, "sh2"), psH2, (t0, t1), pan_cols=DH)
            K.barrier(f"h3a_{hh}")
            h2a.free()
            K.job = w3_job
            ws_layer("w3", w3_p, FH, FCD,
                     lambda kc, ts: h1a.ap(kc * TH + ts * 512, kc * TH + (ts + 1) * 512),
                     out_evac, psH2, (t0, t1), pan_cols=DH)
            K.drain_job()
            K.barrier(f"outw_{hh}")
            for b in (outst[0], outst[1], strb[0], strb[1]):
                b.free()
            strb[0] = strb[1] = None
        h1a.free()

    order = ["prefix", "A", "C", "D"]
    stop = len(order) if upto == "full" else order.index(upto) + 1

    boxes = {}

    def lazy_apply(key, src, mr, sr, aff, tag, cols):
        boxes[key] = AB.alloc(FCD * TH, key)
        yield from apply_job(
            src, mr, sr,
            lambda fc, key=key: boxes[key].ap(fc * TH, (fc + 1) * TH),
            aff, tag, cols)

    if stop >= 1:
        prefix_phase(
            pre_kv_job=None,
            state_job=chain_jobs(
                ln_stats_job(xT, (RW_M1, RW_R1, RW_S12), True, "m1"),
                lazy_apply("xnb0", xT, RW_M1, RW_S12,
                           ("n1w", "n1b"), "xn0", (0, TH))) if stop >= 2 else None)
    if stop >= 2:
        mask_sb = AF32.alloc(16 * C, "mask")
        K.op("sync", lambda e: e.dma_start(out=mask_sb.ap(), in_=mask_c[:, :]),
             inc=(s_vload, 16))
        mask_w = s_vload.n
        attn_half(0, mask_sb, boxes["xnb0"],
                  qkvg_job=lazy_apply("xnb1", xT, RW_M1, RW_S12,
                                      ("n1w", "n1b"), "xn1", (TH, 2 * TH)),
                  mask_w=mask_w)
        attn_half(1, mask_sb, boxes["xnb1"],
                  qkvg_job=lazy_apply("x2b0", z_scr, RW_MZ, RW_SZ,
                                      ("n2w", "n2b"), "x2h0", (0, TH)),
                  mask_w=mask_w)
        if "Af" in tap_ap:
            tap_gate()
            K.op("sync", lambda e: e.dma_start(out=tap_ap["Af"][:, :], in_=A_f),
                 inc=(s_store, 16))
            tap_sync()
        for b in (mask_sb,):
            b.free()
    if stop >= 3:
        A_fb.free()
        A_bb.free()
        gp_sb.free()
        gq_sb.free()
        gcpg_sb.free()
        phaseC_half(0, boxes["x2b0"],
                    ug_job=lazy_apply("x2b1", z_scr, RW_MZ, RW_SZ,
                                      ("n2w", "n2b"), "x2h1", (TH, 2 * TH)))
        phaseC_half(1, boxes["x2b1"],
                    ug_job=lazy_apply("y2b0", x1_scr, RW_MY, RW_RY,
                                      None, "y2h0", (0, TH)))
    if stop >= 4:
        phaseD_half(0, boxes["y2b0"],
                    w3_job=lazy_apply("y2b1", x1_scr, RW_MY, RW_RY,
                                      None, "y2h1", (TH, 2 * TH)))
        phaseD_half(1, boxes["y2b1"])

    K.op("sync", lambda e: e.nop(),
         waits=[(s_store, s_store.n), (s_sp[0], 16 * sp_rnd[0]), (s_sp[1], 16 * sp_rnd[1]),
                (s_row, s_row.n)])
    K.barrier("end")
    K.finalize()
    es.close()
    return nc


_PREP = {}


def prep_weights(inputs):
    key = id(inputs.get("L0_attn_wq"))
    if _PREP.get("key") == key:
        return _PREP["data"]
    f32 = np.float32
    dmask, ident, selc = host_consts()
    w_in = np.asarray(inputs["L0_attn_w_in"], f32)
    b_in = np.asarray(inputs["L0_attn_b_in"], f32)
    w_out = np.asarray(inputs["L0_attn_w_out"], f32)
    winu = np.zeros((D, HPAD), f32)
    winu[:, :HIN] = w_in[:, :HIN]
    wing = np.zeros((D, HPAD), f32)
    wing[:, :HIN] = w_in[:, HIN:]
    binu = np.zeros(HPAD, f32)
    binu[:HIN] = b_in[:HIN]
    bing = np.zeros(HPAD, f32)
    bing[:HIN] = b_in[HIN:]
    woutp = np.zeros((HPAD, D), f32)
    woutp[:HIN, :] = w_out

    vecs = dict(
        bq=inputs["L0_attn_bq"], bk=inputs["L0_attn_bk"], bv=inputs["L0_attn_bv"],
        bg=inputs["L0_attn_bg"], bo=inputs["L0_attn_bo"],
        n1w=inputs["L0_attn_norm1_w"], n1b=inputs["L0_attn_norm1_b"],
        gnw=inputs["L0_attn_gn_w"], gnb=inputs["L0_attn_gn_b"],
        n2w=inputs["L0_attn_norm2_w"], n2b=inputs["L0_attn_norm2_b"],
        b_out=inputs["L0_attn_b_out"], binu=binu, bing=bing,
        s1=inputs["L0_ffn_w1_s"], s2=inputs["L0_ffn_w2_s"],
        sh1=inputs["L0_ffn_h1_s"], sh2=inputs["L0_ffn_h2_s"], s3=inputs["L0_ffn_w3_s"],
    )
    vpk = np.zeros((128, NVCOL), f32)
    for nme, ncc in zip(VEC_NAMES, VEC_NCH):
        v = np.asarray(vecs[nme], f32).reshape(ncc, 128).T
        vpk[:, VCOL[nme]:VCOL[nme] + ncc] = v

    data = dict(
        wq_p=pack_ws(np.asarray(inputs["L0_attn_wq"], f32)),
        wk_p=pack_ws(np.asarray(inputs["L0_attn_wk"], f32)),
        wv_p=pack_ws(np.asarray(inputs["L0_attn_wv"], f32)),
        wg_p=pack_ws(np.asarray(inputs["L0_attn_wg"], f32)),
        wo_p=pack_ws(np.asarray(inputs["L0_attn_wo"], f32)),
        win_u=pack_ws(winu), win_g=pack_ws(wing),
        wout_p=pack_ws(woutp),
        w1_p=pack_ws(np.asarray(inputs["L0_ffn_w1_w"], f32).T),
        w2_p=pack_ws(np.asarray(inputs["L0_ffn_w2_w"], f32).T),
        h1_p=pack_ws(np.asarray(inputs["L0_ffn_h1_w"], f32).T),
        h2_p=pack_ws(np.asarray(inputs["L0_ffn_h2_w"], f32).T),
        w3_p=pack_ws(np.asarray(inputs["L0_ffn_w3_w"], f32).T),
        vpack=vpk,
        brows=np.stack([np.asarray(inputs["L0_attn_bk"], f32),
                        np.asarray(inputs["L0_attn_bv"], f32)]),
        mask_c=dmask, ident_c=ident, selc_c=selc,
    )
    _PREP["key"] = key
    _PREP["data"] = data
    return data


_NC_CACHE = {}


def get_nc(upto="full", taps=()):
    key = (upto, tuple(sorted(taps)))
    if key not in _NC_CACHE:
        _NC_CACHE[key] = build_program(upto, taps)
    return _NC_CACHE[key]


def make_in_maps(inputs):
    x = np.asarray(inputs["L0_attn_x"], np.float32)
    shared = prep_weights(inputs)
    in_maps = []
    for c in range(8):
        b, s = c // 2, c % 2
        im = dict(shared)
        im["xT"] = np.ascontiguousarray(x[b, s * T:(s + 1) * T, :].T)
        im["xTpre"] = np.ascontiguousarray(x[b, 0:T, :].T)
        im["gate_in"] = np.full(128, float(s), np.float32)
        in_maps.append(im)
    return in_maps


def kernel(**inputs):
    nc = get_nc()
    in_maps = make_in_maps(inputs)
    res = run_bass_kernel_spmd(nc, in_maps, core_ids=list(range(8)))
    odt = np.asarray(res.results[0]["out"]).dtype
    out = np.empty((4, 4096, 1024), odt)
    for c in range(8):
        b, s = c // 2, c % 2
        out[b, s * T:(s + 1) * T, :] = res.results[c]["out"].T
    return out


# version 18 (final)
# speedup vs baseline: 1.0293x; 1.0293x over previous
import contextlib
import numpy as np
import ml_dtypes
import concourse.bass as bass
import concourse.mybir as mybir
from concourse.bass_utils import run_bass_kernel_spmd

F32 = mybir.dt.float32
BF16 = mybir.dt.bfloat16
AF = mybir.ActivationFunctionType
ALU = mybir.AluOpType
AX = mybir.AxisListType

D, FCD, T = 1024, 8, 2048
C, NCH, H, HD = 128, 16, 16, 64
HIN, HPAD, KCI = 2730, 2816, 22
DH, FH = 4096, 32
EPS = 1e-5
TH = 1024
NHALF = 2
ENGINES = ("tensor", "vector", "scalar", "sync", "gpsimd")

RW_M1, RW_R1, RW_S12, RW_MZ, RW_SZ, RW_MY, RW_RY, RW_TMP = range(8)
RET_LEVEL = 5


class Sem:
    __slots__ = ("h", "n", "name")

    def __init__(self, h, name):
        self.h = h
        self.n = 0
        self.name = name


class KB:
    def __init__(self, nc, es):
        self.nc = nc
        self.es = es
        self.ops = {e: [] for e in ENGINES}
        self.nsem = 0
        self.store_sem = None
        self.extra_barrier_waits = None
        self.job = None

    def pump(self):
        if self.job is not None:
            try:
                next(self.job)
            except StopIteration:
                self.job = None

    def drain_job(self):
        while self.job is not None:
            self.pump()

    def sem(self, name=None):
        name = name or f"s{self.nsem}"
        self.nsem += 1
        h = self.es.enter_context(self.nc.semaphore(name))
        return Sem(h, name)

    def op(self, eng, fn, waits=(), inc=None):
        ws = tuple((s.h, int(v)) for (s, v) in waits if v > 0)
        if inc is not None:
            inc[0].n += inc[1]
            inc = (inc[0].h, inc[1])
        self.ops[eng].append((fn, ws, inc))

    def drain(self, eng, waits=(), inc=None):
        self.op(eng, lambda e: e.drain(), waits=waits, inc=inc)

    def barrier(self, tag):
        if not hasattr(self, "_bar"):
            self._bar = {e: self.sem(f"bar_{e}") for e in ENGINES}
        sems = self._bar
        for e in ENGINES:
            if e in ("sync", "gpsimd"):
                self.op(e, lambda en: en.nop(), inc=(sems[e], 1))
            else:
                self.drain(e, inc=(sems[e], 1))
        waits = [(s, s.n) for s in sems.values()]
        if self.store_sem is not None:
            waits.append((self.store_sem, self.store_sem.n))
        for s_, v_ in (self.extra_barrier_waits() if self.extra_barrier_waits else []):
            waits.append((s_, v_))
        for e in ENGINES:
            self.op(e, lambda en: en.nop(), waits=waits)

    def finalize(self):
        block = self.es.enter_context(self.nc.Block())
        for eng in ENGINES:
            eng_ops = self.ops[eng]

            def body(e, eng_ops=eng_ops):
                for fn, ws, inc in eng_ops:
                    for (h, v) in ws:
                        e.wait_ge(h, v)
                    ins = fn(e)
                    if inc is not None:
                        ins.then_inc(inc[0], inc[1])

            getattr(block, eng)(body)


class Buf:
    __slots__ = ("arena", "off", "n")

    def __init__(self, arena, off, n):
        self.arena = arena
        self.off = off
        self.n = n

    def ap(self, c0=0, c1=None, p0=0, p1=128):
        c1 = self.n if c1 is None else c1
        assert 0 <= c0 <= c1 <= self.n, (c0, c1, self.n)
        return self.arena.t[p0:p1, self.off + c0:self.off + c1]

    def free(self):
        self.arena.free_(self.off, self.n)


class Arena:
    def __init__(self, nc, es, name, dtype, cols):
        self.t = es.enter_context(nc.sbuf_tensor(name, [128, cols], dtype))
        self.name = name
        self.cols = cols
        self.freelist = [[0, cols]]
        self.inuse = 0
        self.peak = 0

    def alloc(self, n, tag=""):
        n = (n + 7) & ~7
        for seg in self.freelist:
            if seg[1] >= n:
                off = seg[0]
                seg[0] += n
                seg[1] -= n
                if seg[1] == 0:
                    self.freelist.remove(seg)
                self.inuse += n
                self.peak = max(self.peak, self.inuse)
                return Buf(self, off, n)
        raise MemoryError(f"arena {self.name}: cannot alloc {n} cols for {tag} "
                          f"(inuse={self.inuse}/{self.cols}, free={self.freelist})")

    def free_(self, off, n):
        self.inuse -= n
        self.freelist.append([off, n])
        self.freelist.sort()
        merged = []
        for seg in self.freelist:
            if merged and merged[-1][0] + merged[-1][1] == seg[0]:
                merged[-1][1] += seg[1]
            else:
                merged.append(seg)
        self.freelist = merged


def _gammas():
    return 1.0 - np.exp2(-5.0 - np.arange(H, dtype=np.float64))


def host_consts():
    import jax
    import jax.numpy as jnp
    gam32 = (1.0 - np.exp2(-5.0 - np.arange(H, dtype=np.float32))).astype(np.float32)
    with jax.default_device(jax.devices("cpu")[0]):
        lng = jnp.log(jnp.asarray(gam32))
        n = jnp.arange(C, dtype=jnp.float32)
        E = np.asarray(jnp.exp(lng[:, None] * n[None, :]))
    ii = np.arange(C)
    tri = ii[None, :] >= ii[:, None]
    dmask = np.zeros((C, 16 * C), dtype=np.float32)
    for h in range(H):
        tile = np.where(tri, E[h][np.abs(ii[None, :] - ii[:, None])], np.float32(0.0))
        dmask[:, h * C:(h + 1) * C] = tile.astype(np.float32) * np.float32(0.125)
    ident = np.eye(C, dtype=np.float32)
    selc = np.zeros((16, 1345), dtype=np.float32)
    for h in range(16):
        selc[h, 64 * h:64 * h + 64] = 1.0
    selc[:, 1024:1088] = 1.0
    for k in range(16):
        b = k // 2
        if k % 2 == 0:
            selc[k, 1088 + np.arange(16) * 8 + b] = 1.0
        else:
            selc[k, 1216 + np.arange(16) * 8 + b] = 1.0
    selc[:, 1344] = gam32
    return dmask, ident, selc


GAMMAS_F = [float(1.0 - 2.0 ** (-5.0 - h)) for h in range(16)]


def pack_ws(w_io):
    K_, N_ = w_io.shape
    KC, OC = K_ // 128, N_ // 128
    return np.ascontiguousarray(
        np.asarray(w_io, dtype=np.float32).reshape(KC, 128, OC, 128)
        .transpose(2, 1, 0, 3).reshape(OC, 128, KC * 128))


VEC_NAMES = ["bq", "bk", "bv", "bg", "bo", "n1w", "n1b", "gnw", "gnb", "n2w", "n2b",
             "b_out", "binu", "bing", "s1", "s2", "sh1", "sh2", "s3"]
VEC_NCH = [FCD] * 12 + [KCI, KCI] + [FH] * 4 + [FCD]
VCOL = {}
_b = 0
for _n, _c in zip(VEC_NAMES, VEC_NCH):
    VCOL[_n] = _b
    _b += _c
NVCOL = _b


def build_program(upto="full", taps=()):
    taps = set(taps)
    debug = bool(taps) or upto != "full"
    nc = bass.Bass("TRN2", target_bir_lowering=False)
    es = contextlib.ExitStack()
    K = KB(nc, es)
    dt = nc.dram_tensor

    xT = dt("xT", [D, T], F32, kind="ExternalInput").ap()
    xTpre = dt("xTpre", [D, T], F32, kind="ExternalInput").ap()
    gate_in = dt("gate_in", [128], F32, kind="ExternalInput").ap()
    wq_p = dt("wq_p", [FCD, 128, D], F32, kind="ExternalInput").ap()
    wk_p = dt("wk_p", [FCD, 128, D], F32, kind="ExternalInput").ap()
    wv_p = dt("wv_p", [FCD, 128, D], F32, kind="ExternalInput").ap()
    wg_p = dt("wg_p", [FCD, 128, D], F32, kind="ExternalInput").ap()
    wo_p = dt("wo_p", [FCD, 128, D], F32, kind="ExternalInput").ap()
    win_u = dt("win_u", [KCI, 128, D], F32, kind="ExternalInput").ap()
    win_g = dt("win_g", [KCI, 128, D], F32, kind="ExternalInput").ap()
    wout_p = dt("wout_p", [FCD, 128, HPAD], F32, kind="ExternalInput").ap()
    w1_p = dt("w1_p", [FH, 128, D], F32, kind="ExternalInput").ap()
    w2_p = dt("w2_p", [FH, 128, D], F32, kind="ExternalInput").ap()
    h1_p = dt("h1_p", [FH, 128, DH], F32, kind="ExternalInput").ap()
    h2_p = dt("h2_p", [FH, 128, DH], F32, kind="ExternalInput").ap()
    w3_p = dt("w3_p", [FCD, 128, DH], F32, kind="ExternalInput").ap()
    vpack = dt("vpack", [128, NVCOL], F32, kind="ExternalInput").ap()
    brows = dt("brows", [2, D], F32, kind="ExternalInput").ap()
    mask_c = dt("mask_c", [C, 16 * C], F32, kind="ExternalInput").ap()
    selc_c = dt("selc_c", [16, 1345], F32, kind="ExternalInput").ap()
    ident_c = dt("ident_c", [C, C], F32, kind="ExternalInput").ap()

    out_d = dt("out", [D, T], F32, kind="ExternalOutput").ap()
    scratch_kind = "ExternalOutput"
    z_scr = dt("z_scr", [D, T], F32, kind=scratch_kind).ap()
    x1_scr = dt("x1_scr", [D, T], F32, kind=scratch_kind).ap()
    stat_scr = dt("stat_scr", [8, T], F32, kind=scratch_kind).ap()
    tap_ap = {}
    for name in taps:
        shp = {"xn": [D, T], "q": [D, T], "k": [D, T], "v": [D, T], "gate": [D, T],
               "p1": [D, T], "x2": [D, T], "pmat": [HPAD, T], "y2": [D, T],
               "h1a": [DH, T], "Af": [128, 512],
               "ret0": [128, 6 * 1024]}[name]
        tap_ap[name] = dt(f"tap_{name}", shp, BF16 if name not in ("Af",) else F32,
                          kind="ExternalOutput").ap()


    g_ = es.enter_context
    vps = g_(nc.sbuf_tensor("vps", [128, NVCOL], F32))
    cpool = g_(nc.sbuf_tensor("cpool", [128, 4], F32))
    gate_s = g_(nc.sbuf_tensor("gate_s", [128, 1], F32))
    ident_s = g_(nc.sbuf_tensor("ident_s", [C, C], BF16))
    ones_bf = g_(nc.sbuf_tensor("ones_bf", [128, 1], BF16))
    ones_f1 = g_(nc.sbuf_tensor("ones_f1", [1, 128], F32))
    rowbuf = g_(nc.sbuf_tensor("rowbuf", [1, T], F32))
    NPAN = 4
    PANW = 2048
    pan = [g_(nc.sbuf_tensor(f"pan{i}", [128, PANW], BF16)) for i in range(NPAN)]
    AB = Arena(nc, es, "AB", BF16, 66176)
    AF32 = Arena(nc, es, "AF32", F32, 12992)

    def vp(name, fc):
        c0 = VCOL[name] + fc
        return vps[:, c0:c0 + 1]

    s_vload = K.sem("vload")
    s_gload = K.sem("gload")
    s_pan = [K.sem(f"pan{i}") for i in range(NPAN)]
    s_panu = [K.sem(f"panu{i}") for i in range(NPAN)]
    NSTR = 2
    s_str = [K.sem(f"str{i}") for i in range(NSTR)]
    s_stru = [K.sem(f"stru{i}") for i in range(NSTR)]
    s_row = K.sem("row")
    s_mm = K.sem("mm")
    s_ev = K.sem("ev")
    s_sc = K.sem("sc")
    s_gp = K.sem("gp")
    s_jstr = [K.sem(f"jstr{i}") for i in range(2)]
    s_jstru = [K.sem(f"jstru{i}") for i in range(2)]
    s_sp = [K.sem(f"sp{i}") for i in range(2)]
    s_store = K.sem("store")
    K.store_sem = s_store
    K.extra_barrier_waits = lambda: [(s_sp[0], 16 * sp_rnd[0]), (s_sp[1], 16 * sp_rnd[1]),
                                     (s_row, s_row.n)]
    pan_rnd = [0] * NPAN
    str_rnd = [0, 0]
    jstr_rnd = [0, 0]
    sp_rnd = [0, 0]
    strb = [None, None]

    def panel_load(dram3, oc, slot, count=1, ncols=None, c0=0, c1=None):
        waits = [(s_panu[slot], s_panu[slot].n)] if pan_rnd[slot] > 0 else []
        c1 = ncols if c1 is None else c1
        w = c1 - c0
        assert count * w <= PANW
        if count == 1:
            K.op("gpsimd",
                 lambda e, d=dram3, o=oc, a=c0, b=c1, sl=slot: e.dma_start(
                     out=pan[sl][:, 0:b - a], in_=d[o, :, a:b]),
                 waits=waits, inc=(s_pan[slot], 16))
        else:
            K.op("gpsimd",
                 lambda e, d=dram3, o=oc, a=c0, b=c1, sl=slot, c=count: e.dma_start(
                     out=pan[sl][:, 0:c * (b - a)].rearrange("p (o n) -> p o n", n=b - a),
                     in_=d[o:o + c, :, a:b].rearrange("o p n -> p o n")),
                 waits=waits, inc=(s_pan[slot], 16))
        pan_rnd[slot] += 1
        return 16 * pan_rnd[slot]

    def stream_rows(dram2, fc, slot, cols=(0, T)):
        waits = [(s_stru[slot], s_stru[slot].n)] if str_rnd[slot] > 0 else []
        c0, c1 = cols
        bb = strb[slot]
        K.op("sync",
             lambda e, d=dram2, f=fc, bb=bb, a=c0, b=c1: e.dma_start(
                 out=bb.ap(0, b - a), in_=d[f * 128:(f + 1) * 128, a:b]),
             waits=waits, inc=(s_str[slot], 16))
        str_rnd[slot] += 1
        return 16 * str_rnd[slot]

    def spill(dram2, fc, src_ap, slot, cols=(0, T), producer_wait=()):
        c0, c1 = cols
        K.op("sync",
             lambda e, d=dram2, f=fc, s=src_ap, a=c0, b=c1: e.dma_start(
                 out=d[f * 128:(f + 1) * 128, a:b], in_=s),
             waits=producer_wait, inc=(s_sp[slot], 16))
        sp_rnd[slot] += 1
        return 16 * sp_rnd[slot]

    def mm(ps, lhs, rhs, start, stop, waits=(), inc=None):
        K.op("tensor",
             lambda e, p=ps, l=lhs, r=rhs, a=start, b=stop: e.matmul(p, l, r, start=a, stop=b),
             waits=waits, inc=inc)

    def row_load(ridx, cols=(0, T)):
        c0, c1 = cols
        K.op("sync", lambda e, r=ridx, a=c0, b=c1: e.dma_start(
            out=rowbuf[0:1, a:b], in_=stat_scr[r:r + 1, a:b]),
            waits=[(s_mm, s_mm.n)], inc=(s_row, 16))
        return s_row.n

    def tap_sync():
        for _e in ("tensor", "vector", "scalar"):
            K.op(_e, lambda en: en.nop(), waits=[(s_store, s_store.n)])

    def tap_gate():
        K.op("sync", lambda en: en.nop(), waits=[(s_store, s_store.n)])

    def bcast_row(ridx, ps, cols=(0, T)):
        rv = row_load(ridx, cols)
        c0, c1 = cols
        n = (c1 - c0) // 512
        for ts in range(n):
            mm(ps[:, ts * 512:(ts + 1) * 512], ones_f1[0:1, :],
               rowbuf[0:1, c0 + ts * 512:c0 + (ts + 1) * 512], True, True,
               waits=[(s_row, rv), (s_ev, s_ev.n), (s_sc, s_sc.n)] if ts == 0 else [],
               inc=(s_mm, 1) if ts == n - 1 else None)
        return s_mm.n

    K.op("sync", lambda e: e.dma_start(out=vps[:, :], in_=vpack[:, :]), inc=(s_vload, 16))
    K.op("sync", lambda e: e.dma_start(out=gate_s[:, 0], in_=gate_in[:]), inc=(s_vload, 16))
    K.op("gpsimd", lambda e: e.dma_start(out=ident_s[:, :], in_=ident_c[:, :]), inc=(s_gload, 16))
    K.op("vector", lambda e: e.memset(cpool[:, 0:1], EPS))
    K.op("vector", lambda e: e.memset(cpool[:, 1:2], -0.5))
    K.op("vector", lambda e: e.memset(cpool[:, 2:3], 1.0 / D))
    K.op("vector", lambda e: e.memset(cpool[:, 3:4], 1.0 / HD))
    K.op("vector", lambda e: e.memset(ones_bf[:, :], 1.0))
    K.op("vector", lambda e: e.memset(ones_f1[:, :], 1.0))
    K.drain("vector", inc=(s_ev, 1))
    for _e in ("tensor", "vector", "scalar"):
        K.op(_e, lambda en: en.nop(),
             waits=[(s_vload, s_vload.n), (s_gload, s_gload.n), (s_ev, s_ev.n)])
    K.barrier("init")

    gq_sb = AF32.alloc(FCD * C, "gq")
    gp_sb = AF32.alloc(D, "gp")
    gcpg_sb = AF32.alloc(8 * NCH, "gcpg")
    g16b = AF32.alloc(C, "g16")
    g16db = AF32.alloc(C, "g16d")
    g16sb = AF32.alloc(C, "g16s")
    pweb = AF32.alloc(136, "pwe")
    rhfb = [AF32.alloc(128, f"rhf{i}") for i in range(2)]
    selcb = AF32.alloc(1352, "selc")

    def _gsl(a, b, buf=None):
        return (buf or selcb).ap(a, b, p0=0, p1=16)

    K.op("sync", lambda e: e.dma_start(out=_gsl(0, 1345), in_=selc_c[:, :]),
         inc=(s_vload, 16))
    gld = s_vload.n
    K.op("gpsimd", lambda e: e.tensor_copy(_gsl(0, 1, g16b), _gsl(1344, 1345)),
         waits=[(s_vload, gld)])
    K.op("gpsimd", lambda e: e.memset(_gsl(127, 128, g16db), 1.0))
    K.op("gpsimd", lambda e: e.memset(_gsl(0, 8, pweb), 1.0))
    K.drain("gpsimd")
    for st in range(7):
        a = 1 << st
        K.op("gpsimd", lambda e, a=a: e.tensor_scalar_mul(
            _gsl(a, 2 * a, g16b), _gsl(0, a, g16b), _gsl(a - 1, a, g16b)))
        K.drain("gpsimd")
    for st in range(7):
        a = 1 << st
        K.op("gpsimd", lambda e, a=a: e.tensor_scalar_mul(
            _gsl(128 - 2 * a, 128 - a, g16db), _gsl(128 - a, 128, g16db),
            _gsl(a - 1, a, g16b)))
        K.drain("gpsimd")
    K.op("gpsimd", lambda e: e.tensor_scalar_mul(
        _gsl(0, 128, g16sb), _gsl(0, 128, g16b), 0.125))
    K.op("gpsimd", lambda e: e.tensor_scalar_mul(
        _gsl(8, 16, pweb), _gsl(0, 8, pweb), _gsl(127, 128, g16b)))
    K.op("gpsimd", lambda e: e.tensor_mul(
        _gsl(128, 129, pweb), _gsl(127, 128, g16b), _gsl(127, 128, g16b)))
    K.drain("gpsimd")
    K.op("gpsimd", lambda e: e.tensor_scalar_mul(
        _gsl(16, 32, pweb), _gsl(0, 16, pweb), _gsl(128, 129, pweb)))
    K.op("gpsimd", lambda e: e.tensor_mul(
        _gsl(129, 130, pweb), _gsl(128, 129, pweb), _gsl(128, 129, pweb)))
    K.drain("gpsimd")
    K.op("gpsimd", lambda e: e.tensor_scalar_mul(
        _gsl(32, 64, pweb), _gsl(0, 32, pweb), _gsl(129, 130, pweb)))
    K.op("gpsimd", lambda e: e.tensor_mul(
        _gsl(130, 131, pweb), _gsl(129, 130, pweb), _gsl(129, 130, pweb)))
    K.drain("gpsimd")
    K.op("gpsimd", lambda e: e.tensor_scalar_mul(
        _gsl(64, 128, pweb), _gsl(0, 64, pweb), _gsl(130, 131, pweb)))
    K.drain("gpsimd")
    K.op("gpsimd", lambda e: e.tensor_mul(
        _gsl(0, 128, rhfb[0]), _gsl(0, 128, pweb), _gsl(1088, 1216)))
    K.op("gpsimd", lambda e: e.tensor_mul(
        _gsl(0, 128, rhfb[1]), _gsl(0, 128, pweb), _gsl(1216, 1344)))
    K.drain("gpsimd", inc=(s_gp, 1))
    gv2 = s_gp.n
    with contextlib.ExitStack() as gh:
        ptA = gh.enter_context(nc.psum_tensor("ptA", [128, 512], F32))
        ptB = gh.enter_context(nc.psum_tensor("ptB", [128, 512], F32))
        for j_ in range(8):
            for bk in range(2):
                fc = (0 if bk == 0 else 4) + j_ // 2
                half = j_ % 2
                h = 2 * fc + half
                pt = ptA if bk == 0 else ptB
                mm(pt[64 * half:64 * half + 64,
                      (fc % 4) * 128:(fc % 4) * 128 + 128],
                   _gsl(64 * h, 64 * h + 64), _gsl(0, 128, g16sb), True, True,
                   waits=[(s_gp, gv2)] if (j_ == 0 and bk == 0) else [],
                   inc=(s_mm, 1) if j_ == 7 else None)
        gmq = s_mm.n
        K.op("vector", lambda e: e.tensor_copy(gq_sb.ap(0, 512), ptA[:, :]),
             waits=[(s_mm, gmq)])
        K.op("vector", lambda e: e.tensor_copy(gq_sb.ap(512, 1024), ptB[:, :]))
        K.drain("vector", inc=(s_ev, 1))
        gvq = s_ev.n
        for j_ in range(8):
            for bk in range(2):
                h = (0 if bk == 0 else 8) + j_
                pt = ptA if bk == 0 else ptB
                mm(pt[:, j_ * 64:(j_ + 1) * 64],
                   _gsl(0, 128, g16db), _gsl(64 * h, 64 * h + 64), True, True,
                   waits=[(s_ev, gvq)] if (j_ == 0 and bk == 0) else [],
                   inc=(s_mm, 1) if j_ == 7 else None)
        gmp = s_mm.n
        K.op("vector", lambda e: e.tensor_copy(gp_sb.ap(0, 512), ptA[:, :]),
             waits=[(s_mm, gmp)])
        K.op("vector", lambda e: e.tensor_copy(gp_sb.ap(512, 1024), ptB[:, :]))
        K.drain("vector", inc=(s_ev, 1))
        gvp = s_ev.n
        mm(ptA[0:64, 0:128], _gsl(1024, 1088), _gsl(0, 128, rhfb[0]), True, True,
           waits=[(s_ev, gvp)], inc=(s_mm, 1))
        mm(ptB[64:128, 0:128], _gsl(1024, 1088), _gsl(0, 128, rhfb[1]), True, True,
           inc=(s_mm, 1))
        gmc = s_mm.n
        K.op("vector", lambda e: e.tensor_copy(
            gcpg_sb.ap(0, 128, p0=0, p1=64), ptA[0:64, 0:128]),
            waits=[(s_mm, gmc)])
        K.op("vector", lambda e: e.tensor_copy(
            gcpg_sb.ap(0, 128, p0=64, p1=128), ptB[64:128, 0:128]))
        K.drain("vector", inc=(s_ev, 1))
    gen_done = s_ev.n
    K.barrier("gen")
    for _b in (g16b, g16db, g16sb, pweb, rhfb[0], rhfb[1], selcb):
        _b.free()

    def row_math(pstm, pstv, rows_out, double, tag, cols=(0, T), wait0=()):
        c0, c1 = cols
        w = c1 - c0
        rM = AF32.alloc(w, "rm_rM")
        rA = AF32.alloc(w, "rm_rA")
        rB = AF32.alloc(w, "rm_rB")
        rMa, rAa, rBa = rM.ap(p1=1), rA.ap(p1=1), rB.ap(p1=1)
        K.op("vector", lambda e: e.tensor_scalar_mul(rMa, pstm[0:1, 0:w], 1.0 / D),
             waits=list(wait0))
        K.op("vector", lambda e: e.tensor_scalar_mul(rAa, pstv[0:1, 0:w], 1.0 / D))
        K.drain("vector")
        K.op("vector", lambda e: e.tensor_mul(rBa, rMa, rMa))
        K.drain("vector")
        K.op("vector", lambda e: e.tensor_sub(rAa, rAa, rBa))
        K.drain("vector", inc=(s_ev, 1))
        ev0 = s_ev.n
        K.op("sync", lambda e, r=rows_out[0]: e.dma_start(
            out=stat_scr[r:r + 1, c0:c1], in_=rMa), waits=[(s_ev, ev0)], inc=(s_row, 16))
        rspill = s_row.n
        K.op("scalar", lambda e: e.activation(rBa, rAa, AF.Ln, bias=cpool[0:1, 0:1]),
             waits=[(s_ev, ev0)])
        K.drain("scalar")
        K.op("scalar", lambda e: e.mul(rBa, rBa, cpool[0:1, 1:2]))
        K.drain("scalar")
        K.op("scalar", lambda e: e.activation(rBa, rBa, AF.Exp))
        K.drain("scalar", inc=(s_sc, 1))
        sc1 = s_sc.n
        if double:
            K.op("vector", lambda e: e.tensor_mul(rMa, rAa, rBa),
                 waits=[(s_sc, sc1), (s_row, rspill)])
            K.drain("vector")
            K.op("vector", lambda e: e.tensor_mul(rMa, rMa, rBa))
            K.drain("vector", inc=(s_ev, 1))
            ev1 = s_ev.n
            K.op("scalar", lambda e: e.activation(rMa, rMa, AF.Ln, bias=cpool[0:1, 0:1]),
                 waits=[(s_ev, ev1)])
            K.drain("scalar")
            K.op("scalar", lambda e: e.mul(rMa, rMa, cpool[0:1, 1:2]))
            K.drain("scalar")
            K.op("scalar", lambda e: e.activation(rMa, rMa, AF.Exp))
            K.drain("scalar", inc=(s_sc, 1))
            sc2_ = s_sc.n
            K.op("vector", lambda e: e.tensor_mul(rAa, rBa, rMa), waits=[(s_sc, sc2_)])
            K.drain("vector", inc=(s_ev, 1))
            evf = s_ev.n
            rows = [(rows_out[1], rBa, [(s_ev, evf)]), (rows_out[2], rAa, [])]
        else:
            rows = [(rows_out[1], rBa, [(s_sc, sc1)])]
        for ridx, ap_, w0 in rows:
            K.op("sync", lambda e, r=ridx, a=ap_: e.dma_start(
                out=stat_scr[r:r + 1, c0:c1], in_=a), waits=w0, inc=(s_row, 16))
        K.op("sync", lambda e: e.nop(), waits=[(s_row, s_row.n)])
        for b in (rM, rA, rB):
            b.free()

    def ln_stats(src_dram, rows_out, double, tag):
        xb = AB.alloc(T, "st_xb")
        sq = AB.alloc(T, "st_sq")
        strb[0] = AF32.alloc(T, "st_s0")
        strb[1] = AF32.alloc(T, "st_s1")
        with contextlib.ExitStack() as ph:
            p_ = ph.enter_context
            pstm = p_(nc.psum_tensor(f"pstm_{tag}", [1, T], F32))
            pstv = p_(nc.psum_tensor(f"pstv_{tag}", [1, T], F32))
            for fc in range(FCD):
                sl = fc % NSTR
                wv = stream_rows(src_dram, fc, sl)
                bb = strb[sl]
                K.op("scalar", lambda e, bb=bb: e.copy(xb.ap(), bb.ap()),
                     waits=[(s_str[sl], wv), (s_mm, s_mm.n)], inc=(s_stru[sl], 1))
                K.drain("scalar", inc=(s_sc, 1))
                cpd = s_sc.n
                K.op("vector", lambda e: e.tensor_mul(sq.ap(), xb.ap(), xb.ap()),
                     waits=[(s_sc, cpd)])
                K.drain("vector", inc=(s_ev, 1))
                sqd = s_ev.n
                for ts in range(4):
                    mm(pstm[0:1, ts * 512:(ts + 1) * 512], ones_bf[:, 0:1],
                       xb.ap(ts * 512, (ts + 1) * 512), fc == 0, fc == FCD - 1,
                       waits=[(s_sc, cpd)] if ts == 0 else [])
                for ts in range(4):
                    mm(pstv[0:1, ts * 512:(ts + 1) * 512], ones_bf[:, 0:1],
                       sq.ap(ts * 512, (ts + 1) * 512), fc == 0, fc == FCD - 1,
                       waits=[(s_ev, sqd)] if ts == 0 else [],
                       inc=(s_mm, 1) if ts == 3 else None)
            mm_done = s_mm.n
            row_math(pstm, pstv, rows_out, double, tag, (0, T),
                     wait0=[(s_mm, mm_done)])
        for b in (xb, sq, strb[0], strb[1]):
            b.free()
        strb[0] = strb[1] = None
        K.barrier(f"st_{tag}")

    def ln_stats_job(src_dram, rows_out, double, tag):
        xb = AB.alloc(TH, f"stj_xb_{tag}")
        sq = AB.alloc(TH, f"stj_sq_{tag}")
        jb = [AF32.alloc(TH, f"stj_s0_{tag}"), AF32.alloc(TH, f"stj_s1_{tag}")]
        es_j = contextlib.ExitStack()
        pstm = es_j.enter_context(nc.psum_tensor(f"jpstm_{tag}", [1, TH], F32))
        pstv = es_j.enter_context(nc.psum_tensor(f"jpstv_{tag}", [1, TH], F32))
        yield
        for ch in range(2):
            c0, c1 = ch * TH, (ch + 1) * TH
            for fc in range(FCD):
                sl = fc % 2
                bb = jb[sl]
                waits = [(s_jstru[sl], s_jstru[sl].n)] if jstr_rnd[sl] > 0 else []
                K.op("sync",
                     lambda e, f=fc, bb=bb, a=c0, b=c1: e.dma_start(
                         out=bb.ap(0, TH), in_=src_dram[f * 128:(f + 1) * 128, a:b]),
                     waits=waits, inc=(s_jstr[sl], 16))
                jstr_rnd[sl] += 1
                wv = 16 * jstr_rnd[sl]
                K.op("scalar", lambda e, bb=bb: e.copy(xb.ap(), bb.ap()),
                     waits=[(s_jstr[sl], wv), (s_mm, s_mm.n)], inc=(s_jstru[sl], 1))
                K.drain("scalar", inc=(s_sc, 1))
                cpd = s_sc.n
                K.op("vector", lambda e: e.tensor_mul(sq.ap(), xb.ap(), xb.ap()),
                     waits=[(s_sc, cpd)])
                K.drain("vector", inc=(s_ev, 1))
                sqd = s_ev.n
                for ts in range(2):
                    mm(pstm[0:1, ts * 512:(ts + 1) * 512], ones_bf[:, 0:1],
                       xb.ap(ts * 512, (ts + 1) * 512), fc == 0, fc == FCD - 1,
                       waits=[(s_sc, cpd)] if ts == 0 else [])
                for ts in range(2):
                    mm(pstv[0:1, ts * 512:(ts + 1) * 512], ones_bf[:, 0:1],
                       sq.ap(ts * 512, (ts + 1) * 512), fc == 0, fc == FCD - 1,
                       waits=[(s_ev, sqd)] if ts == 0 else [],
                       inc=(s_mm, 1) if ts == 1 else None)
                yield
            row_math(pstm, pstv, rows_out, double, f"{tag}_{ch}", (c0, c1),
                     wait0=[(s_mm, s_mm.n)])
            yield
        es_j.close()
        for b in (xb, sq, jb[0], jb[1]):
            b.free()

    def apply_ln(src_dram, m_row, s_row_i, dst_fn, affine, tag, cols=(0, T)):
        c0, c1 = cols
        w = c1 - c0
        strb[0] = AF32.alloc(w, f"ap_s0_{tag}")
        strb[1] = AF32.alloc(w, f"ap_s1_{tag}")
        tf = AF32.alloc(w, f"ap_tf_{tag}")
        with contextlib.ExitStack() as ph:
            p_ = ph.enter_context
            psA = p_(nc.psum_tensor(f"psA_{tag}", [128, w], F32))
            psB = p_(nc.psum_tensor(f"psB_{tag}", [128, w], F32))
            bcast_row(m_row, psA, cols)
            bc_done = bcast_row(s_row_i, psB, cols)
            for fc in range(FCD):
                sl = fc % NSTR
                wv = stream_rows(src_dram, fc, sl, cols)
                bb = strb[sl]
                K.op("vector", lambda e, bb=bb, w=w: e.tensor_sub(
                    tf.ap(0, w), bb.ap(0, w), psA[:, 0:w]),
                    waits=[(s_str[sl], wv), (s_mm, bc_done),
                           (s_sc, s_sc.n) if affine else (s_ev, s_ev.n)],
                    inc=(s_stru[sl], 1))
                K.drain("vector")
                if affine:
                    K.op("vector", lambda e, w=w: e.tensor_mul(tf.ap(0, w), tf.ap(0, w),
                                                               psB[:, 0:w]),
                         inc=(s_ev, 1))
                    evd = s_ev.n
                    wn, bn = affine
                    K.op("scalar", lambda e, fc=fc, w=w, wn=wn, bn=bn: e.activation(
                        dst_fn(fc), tf.ap(0, w), AF.Identity, bias=vp(bn, fc), scale=vp(wn, fc)),
                        waits=[(s_ev, evd)], inc=(s_sc, 1))
                else:
                    K.op("vector", lambda e, fc=fc, w=w: e.tensor_mul(
                        dst_fn(fc), tf.ap(0, w), psB[:, 0:w]), inc=(s_ev, 1))
                    K.drain("vector")
            if affine:
                K.op("tensor", lambda e: e.nop(), waits=[(s_sc, s_sc.n)])
            else:
                K.op("tensor", lambda e: e.nop(), waits=[(s_ev, s_ev.n)])
        for b in (strb[0], strb[1], tf):
            b.free()
        strb[0] = strb[1] = None

    def chain_jobs(*gens):
        for g in gens:
            yield from g

    def apply_job(src_dram, m_row, s_row_i, dst_fn, affine, tag, cols=(0, T)):
        c0, c1 = cols
        w = c1 - c0
        jb = [AF32.alloc(w, f"jb0_{tag}"), AF32.alloc(w, f"jb1_{tag}")]
        tf = AF32.alloc(w, f"jtf_{tag}")
        es_j = contextlib.ExitStack()
        psA = es_j.enter_context(nc.psum_tensor(f"jpsA_{tag}", [128, w], F32))
        psB = es_j.enter_context(nc.psum_tensor(f"jpsB_{tag}", [128, w], F32))
        bcast_row(m_row, psA, cols)
        bc_done = bcast_row(s_row_i, psB, cols)
        yield
        for fc in range(FCD):
            sl = fc % 2
            bb = jb[sl]
            waits = [(s_jstru[sl], s_jstru[sl].n)] if jstr_rnd[sl] > 0 else []
            K.op("sync",
                 lambda e, f=fc, bb=bb, a=c0, b=c1: e.dma_start(
                     out=bb.ap(0, b - a), in_=src_dram[f * 128:(f + 1) * 128, a:b]),
                 waits=waits, inc=(s_jstr[sl], 16))
            jstr_rnd[sl] += 1
            wv = 16 * jstr_rnd[sl]
            K.op("vector", lambda e, bb=bb, w=w: e.tensor_sub(
                tf.ap(0, w), bb.ap(0, w), psA[:, 0:w]),
                waits=[(s_jstr[sl], wv), (s_mm, bc_done),
                       (s_sc, s_sc.n) if affine else (s_ev, s_ev.n)],
                inc=(s_jstru[sl], 1))
            K.drain("vector")
            if affine:
                K.op("vector", lambda e, w=w: e.tensor_mul(tf.ap(0, w), tf.ap(0, w),
                                                           psB[:, 0:w]),
                     inc=(s_ev, 1))
                evd = s_ev.n
                wn, bn = affine
                K.op("scalar", lambda e, fc=fc, w=w, wn=wn, bn=bn: e.activation(
                    dst_fn(fc), tf.ap(0, w), AF.Identity, bias=vp(bn, fc),
                    scale=vp(wn, fc)),
                    waits=[(s_ev, evd)], inc=(s_sc, 1))
            else:
                K.op("vector", lambda e, fc=fc, w=w: e.tensor_mul(
                    dst_fn(fc), tf.ap(0, w), psB[:, 0:w]), inc=(s_ev, 1))
                K.drain("vector")
            yield
        es_j.close()
        for b in (jb[0], jb[1], tf):
            b.free()

    def ws_layer(tag, panels, KC, OC, rhs_fn, evac, psG, cols=(0, T), pan_cols=None):
        t0, t1 = cols
        w = t1 - t0
        nts = w // 512
        pan_cols = pan_cols or KC * 128
        NG = len(psG)
        grp_free = [None] * NG
        if pan_cols <= PANW:
            G = PANW // pan_cols
            ngrp = (OC + G - 1) // G
            gw = []
            for gi in range(min(3, ngrp)):
                gw.append(panel_load(panels, gi * G, gi % NPAN,
                                     min(G, OC - gi * G), pan_cols))
            for oc in range(OC):
                g = oc % NG
                gi = oc // G
                slot = gi % NPAN
                off = (oc % G) * pan_cols
                waits = []
                if oc % G == 0:
                    waits.append((s_pan[slot], gw[gi]))
                if grp_free[g] is not None:
                    waits.append(grp_free[g])
                first = True
                grp_last = (oc % G == G - 1) or (oc == OC - 1)
                for kc in range(KC):
                    for ts in range(nts):
                        last = (kc == KC - 1 and ts == nts - 1)
                        inc = None
                        if last:
                            inc = (s_panu[slot], 1) if grp_last else (s_mm, 1)
                        mm(psG[g][:, ts * 512:(ts + 1) * 512],
                           pan[slot][:, off + kc * 128:off + (kc + 1) * 128],
                           rhs_fn(kc, ts),
                           kc == 0, kc == KC - 1,
                           waits=waits if first else [], inc=inc)
                        first = False
                mm_done = ((s_panu[slot], s_panu[slot].n) if grp_last
                           else (s_mm, s_mm.n))
                if oc % G == 0 and gi + 3 < ngrp:
                    gw.append(panel_load(panels, (gi + 3) * G, (gi + 3) % NPAN,
                                         min(G, OC - (gi + 3) * G), pan_cols))
                grp_free[g] = evac(oc, psG[g], mm_done)
                K.pump()
        else:
            KSPLIT = PANW // 128
            pw = []

            def load_pair(oc, pi):
                w0 = panel_load(panels, oc, 2 * pi, 1, None, 0, PANW)
                w1 = panel_load(panels, oc, 2 * pi + 1, 1, None, PANW, pan_cols)
                return (w0, w1)

            for oc in range(min(2, OC)):
                pw.append(load_pair(oc, oc % 2))
            for oc in range(OC):
                g = oc % NG
                pi = oc % 2
                waits = [(s_pan[2 * pi], pw[oc][0]), (s_pan[2 * pi + 1], pw[oc][1])]
                if grp_free[g] is not None:
                    waits.append(grp_free[g])
                first = True
                for kc in range(KC):
                    sl = 2 * pi + (1 if kc >= KSPLIT else 0)
                    off = (kc - KSPLIT) * 128 if kc >= KSPLIT else kc * 128
                    for ts in range(nts):
                        inc = None
                        if kc == KSPLIT - 1 and ts == nts - 1:
                            inc = (s_panu[2 * pi], 1)
                        elif kc == KC - 1 and ts == nts - 1:
                            inc = (s_panu[2 * pi + 1], 1)
                        mm(psG[g][:, ts * 512:(ts + 1) * 512],
                           pan[sl][:, off:off + 128],
                           rhs_fn(kc, ts),
                           kc == 0, kc == KC - 1,
                           waits=waits if first else [], inc=inc)
                        first = False
                mm_done = (s_panu[2 * pi + 1], s_panu[2 * pi + 1].n)
                if oc + 2 < OC:
                    pw.append(load_pair(oc + 2, oc % 2))
                grp_free[g] = evac(oc, psG[g], mm_done)
                K.pump()
        for g in range(NG):
            if grp_free[g] is not None:
                K.op("tensor", lambda e: e.nop(), waits=[grp_free[g]])

    A_fb = AF32.alloc(512, "A_f")
    A_bb = AB.alloc(512, "A_b")
    A_f, A_b = A_fb.ap(), A_bb.ap()

    def prefix_phase(pre_kv_job=None, state_job=None):
        ln_stats(xTpre, (RW_MZ, RW_SZ, RW_TMP), True, "pre")
        xnp = AB.alloc(FCD * T, "xnp")
        apply_ln(xTpre, RW_MZ, RW_TMP, lambda fc: xnp.ap(fc * T, (fc + 1) * T),
                 ("n1w", "n1b"), "preap")
        K.barrier("preap")
        kpre = AB.alloc(FCD * T, "kpre")
        vpre = AB.alloc(FCD * T, "vpre")
        with contextlib.ExitStack() as ph:
            p_ = ph.enter_context
            psG = [p_(nc.psum_tensor(f"psP{i}", [128, T], F32)) for i in range(2)]

            def pevac(dstb, bias):
                def evac(oc, ps, mmw):
                    K.op("scalar", lambda e, oc=oc, ps=ps, dstb=dstb, bias=bias:
                         e.activation(dstb.ap(oc * T, (oc + 1) * T), ps[:, :],
                                      AF.Identity, bias=vp(bias, oc)),
                         waits=[mmw], inc=(s_sc, 1))
                    return (s_sc, s_sc.n)
                return evac

            def rhs_xnp(kc, ts):
                return xnp.ap(kc * T + ts * 512, kc * T + (ts + 1) * 512)

            ws_layer("kpre", wk_p, FCD, FCD, rhs_xnp, pevac(kpre, "bk"), psG, (0, T))
            ws_layer("vpre", wv_p, FCD, FCD, rhs_xnp, pevac(vpre, "bv"), psG, (0, T))
        xnp.free()
        K.barrier("prekv")
        K.op("vector", lambda e: e.memset(A_f, 0.0))
        K.drain("vector", inc=(s_ev, 1))
        with contextlib.ExitStack() as ph:
            p_ = ph.enter_context
            psT_p = p_(nc.psum_tensor("psT_p", [128, 1024], BF16))
            st_p = [p_(nc.psum_tensor(f"st_p{i}", [128, 512], F32)) for i in range(2)]
            ktd = AB.alloc(D, "ktd_p")
            vtc = AB.alloc(D, "vtc_p")
            st_free = [0, 0]
            psT_free = 0
            K.job = state_job
            for c in range(NCH):
                cb = (c * C, (c + 1) * C)
                for fc in range(FCD):
                    K.op("tensor", lambda e, fc=fc, cb=cb: e.transpose(
                        psT_p[:, fc * 128:(fc + 1) * 128],
                        vpre.ap(fc * T + cb[0], fc * T + cb[1]), ident_s[:, :]),
                        waits=[(s_ev, psT_free)] if fc == 0 else [],
                        inc=(s_mm, 1) if fc == FCD - 1 else None)
                vtr = s_mm.n
                K.op("vector", lambda e: e.tensor_copy(vtc.ap(), psT_p[:, :]),
                     waits=[(s_mm, vtr)], inc=(s_ev, 1))
                vtc_ready = s_ev.n
                for fc in range(FCD):
                    K.op("tensor", lambda e, fc=fc, cb=cb: e.transpose(
                        psT_p[:, fc * 128:(fc + 1) * 128],
                        kpre.ap(fc * T + cb[0], fc * T + cb[1]), ident_s[:, :]),
                        waits=[(s_ev, vtc_ready)] if fc == 0 else [],
                        inc=(s_mm, 1) if fc == FCD - 1 else None)
                ktr = s_mm.n
                K.op("vector", lambda e: e.tensor_mul(ktd.ap(), psT_p[:, :], gp_sb.ap()),
                     waits=[(s_mm, ktr)], inc=(s_ev, 1))
                psT_free = s_ev.n
                ktd_ready = s_ev.n
                for w in range(4):
                    for i in range(4):
                        h = 4 * w + i
                        po = 64 * (h % 2)
                        col = 128 * w + 64 * ((h // 2) % 2)
                        mm(st_p[c % 2][po:po + 64, col:col + 64],
                           ktd.ap(h * 64, (h + 1) * 64),
                           vtc.ap(h * 64, (h + 1) * 64), True, True,
                           waits=[(s_ev, ktd_ready), (s_ev, st_free[c % 2])]
                           if (w == 0 and i == 0) else [],
                           inc=(s_mm, 1) if h == H - 1 else None)
                st_done = s_mm.n
                K.drain("vector")
                for b in range(8):
                    K.op("vector", lambda e, b=b, c=c: e.scalar_tensor_tensor(
                        A_fb.ap(64 * b, 64 * b + 64),
                        st_p[c % 2][:, 64 * b:64 * b + 64],
                        gcpg_sb.ap((15 - c) * 8 + b, (15 - c) * 8 + b + 1),
                        A_fb.ap(64 * b, 64 * b + 64),
                        ALU.mult, ALU.add),
                        waits=[(s_mm, st_done)] if b == 0 else [])
                K.drain("vector", inc=(s_ev, 1))
                st_free[c % 2] = s_ev.n
                K.pump()
            K.drain_job()
            K.op("tensor", lambda e: e.nop(), waits=[(s_ev, s_ev.n)])
            ktd.free()
            vtc.free()
        kpre.free()
        vpre.free()
        K.op("vector", lambda e: e.tensor_scalar_mul(A_f, A_f, gate_s[:, 0:1]))
        K.drain("vector")
        K.op("vector", lambda e: e.tensor_copy(A_b, A_f))
        K.drain("vector", inc=(s_ev, 1))
        K.barrier("prefix")

    ret_state = dict(sc_free=[0, 0], smb_free=[0, 0], rt_free=[0, 0], st_free=[0, 0],
                     gsq_done=0,
                     aupd_blk=[0] * 4, gn_evac=0, psT_free=0, rtt_free=[0, 0],
                     pend_vec=[], pend_tail=None, gnn_done=0, gsc=0)

    def attn_half(hh, mask_sb, xn, qkvg_job=None):
        t0, t1 = hh * TH, (hh + 1) * TH
        if hh == 0 and "xn" in tap_ap:
            tap_gate()
            for fc in range(FCD):
                K.op("sync", lambda e, fc=fc: e.dma_start(
                    out=tap_ap["xn"][fc * 128:(fc + 1) * 128, 0:TH],
                    in_=xn.ap(fc * TH, (fc + 1) * TH)), inc=(s_store, 16))
            tap_sync()
        qb = AB.alloc(FCD * TH, "qb")
        kb = AB.alloc(FCD * TH, "kb")
        vb = AB.alloc(FCD * TH, "vb")
        gb = AB.alloc(FCD * TH, "gb")
        with contextlib.ExitStack() as ph:
            p_ = ph.enter_context
            psG = [p_(nc.psum_tensor(f"psQ{i}_{hh}", [128, TH], F32)) for i in range(2)]

            def qevac(dstb, bias, func=AF.Identity):
                def evac(oc, ps, mmw):
                    K.op("scalar", lambda e, oc=oc, ps=ps: e.activation(
                        dstb.ap(oc * TH, (oc + 1) * TH), ps[:, :], func, bias=vp(bias, oc)),
                        waits=[mmw], inc=(s_sc, 1))
                    return (s_sc, s_sc.n)
                return evac

            def rhs_xn(kc, ts):
                return xn.ap(kc * TH + ts * 512, kc * TH + (ts + 1) * 512)

            K.job = qkvg_job
            ws_layer("q", wq_p, FCD, FCD, rhs_xn, qevac(qb, "bq"), psG, (t0, t1))
            ws_layer("k", wk_p, FCD, FCD, rhs_xn, qevac(kb, "bk"), psG, (t0, t1))
            ws_layer("v", wv_p, FCD, FCD, rhs_xn, qevac(vb, "bv"), psG, (t0, t1))
            ws_layer("g", wg_p, FCD, FCD, rhs_xn, qevac(gb, "bg", AF.Silu), psG, (t0, t1))
            K.drain_job()
        xn.free()
        K.barrier(f"qkvg{hh}")
        if hh == 0:
            for nm, bb in (("q", qb), ("k", kb), ("v", vb), ("gate", gb)):
                if nm in tap_ap:
                    tap_gate()
                    for fc in range(FCD):
                        K.op("sync", lambda e, fc=fc, bb=bb, nm=nm: e.dma_start(
                            out=tap_ap[nm][fc * 128:(fc + 1) * 128, 0:TH],
                            in_=bb.ap(fc * TH, (fc + 1) * TH)), inc=(s_store, 16))
                    tap_sync()
        S = ret_state
        NLC = NCH // NHALF
        with contextlib.ExitStack() as ph:
            p_ = ph.enter_context
            sc2 = [p_(nc.psum_tensor(f"sc2_{i}_{hh}", [128, 512], F32)) for i in range(2)]
            rt2 = [p_(nc.psum_tensor(f"rt2_{i}_{hh}", [128, 512], F32)) for i in range(2)]
            st2 = [p_(nc.psum_tensor(f"st2_{i}_{hh}", [128, 512], F32)) for i in range(2)]
            psT = p_(nc.psum_tensor(f"psT_{hh}", [128, 1024], BF16))
            psT2 = p_(nc.psum_tensor(f"psT2_{hh}", [128, 1024], BF16))
            qsb = [AB.alloc(D, f"qs{i}") for i in range(2)]
            ktd = AB.alloc(D, "ktd")
            vtc = AB.alloc(D, "vtc")
            smb = [AB.alloc(512, f"smb{i}") for i in range(2)]
            rttb = [AB.alloc(D, f"rtt{i}") for i in range(2)]
            gsq = AB.alloc(D, "gsq")
            gnn = AB.alloc(D, "gnn")
            gnf = AB.alloc(C, "gnf")
            gnst = AF32.alloc(64, "gnst")
            qs_ready = [0, 0]

            def emit_qsks(lc, cp):
                cb = (lc * C, (lc + 1) * C)
                for fc in range(FCD):
                    K.op("vector", lambda e, fc=fc, cb=cb, cp=cp: e.tensor_mul(
                        qsb[cp].ap(fc * C, (fc + 1) * C),
                        qb.ap(fc * TH + cb[0], fc * TH + cb[1]),
                        gq_sb.ap(fc * C, (fc + 1) * C)),
                        waits=[(s_mm, s_mm.n)] if fc == 0 else [])
                K.drain("vector", inc=(s_ev, 1))
                qs_ready[cp] = s_ev.n

            emit_qsks(0, 0)
            for lc in range(NLC):
                cp = lc % 2
                cb = (lc * C, (lc + 1) * C)
                sc_done = [0, 0]
                mask_done = [0, 0]
                rcs_done = [0, 0]

                def emit_score(w, i, extra=()):
                    h = 4 * w + i
                    fc, po = h // 2, 64 * (h % 2)
                    wb = w % 2
                    mm(sc2[wb][:, 128 * i:128 * i + 128],
                       kb.ap(fc * TH + cb[0], fc * TH + cb[1], p0=po, p1=po + 64),
                       qb.ap(fc * TH + cb[0], fc * TH + cb[1], p0=po, p1=po + 64),
                       True, True, waits=list(extra),
                       inc=(s_mm, 1) if i == 3 else None)

                def emit_mask(w):
                    wb = w % 2
                    K.op("vector", lambda e, wb=wb, w=w: e.tensor_mul(
                        smb[wb].ap(), sc2[wb][:, :], mask_sb.ap(512 * w, 512 * (w + 1))),
                        waits=[(s_mm, sc_done[wb]), (s_mm, S["smb_free"][wb])],
                        inc=(s_ev, 1))
                    mask_done[wb] = s_ev.n
                    S["sc_free"][wb] = s_ev.n

                def emit_rcs(w, i, extra=()):
                    h = 4 * w + i
                    fc, po = h // 2, 64 * (h % 2)
                    wb = w % 2
                    ro = 256 * (w // 2) + 64 * i
                    mm(rt2[wb][:, ro:ro + 64],
                       qsb[cp].ap(fc * C, (fc + 1) * C, p0=po, p1=po + 64),
                       A_bb.ap((h // 2) * 64, (h // 2) * 64 + 64, p0=po, p1=po + 64),
                       True, False, waits=list(extra))
                    mm(rt2[wb][:, ro:ro + 64],
                       smb[wb].ap(128 * i, 128 * i + 128),
                       vtc.ap(h * 64, (h + 1) * 64), False, True)
                    col = 128 * (w // 2) + 64 * ((h // 2) % 2)
                    mm(st2[wb][po:po + 64, col:col + 64],
                       ktd.ap(h * 64, (h + 1) * 64),
                       vtc.ap(h * 64, (h + 1) * 64), True, True,
                       inc=(s_mm, 1) if i == 3 else None)

                tw = [(s_ev, S["psT_free"]), (s_mm, s_mm.n)]
                for fc in range(FCD):
                    K.op("tensor", lambda e, fc=fc, cb=cb: e.transpose(
                        psT[:, fc * 128:(fc + 1) * 128],
                        vb.ap(fc * TH + cb[0], fc * TH + cb[1]), ident_s[:, :]),
                        waits=tw if fc == 0 else [],
                        inc=(s_mm, 1) if fc == FCD - 1 else None)
                    if fc < 4:
                        ex = ([(s_ev, S["sc_free"][0])] if fc == 0 else [])
                        emit_score(0, fc, ex)
                vtr = s_mm.n
                sc_done[0] = s_mm.n
                emit_mask(0)
                K.op("vector", lambda e: e.tensor_copy(vtc.ap(), psT[:, :]),
                     waits=[(s_mm, vtr)], inc=(s_ev, 1))
                vtc_ready = s_ev.n
                for fc in range(FCD):
                    K.op("tensor", lambda e, fc=fc, cb=cb: e.transpose(
                        psT[:, fc * 128:(fc + 1) * 128],
                        kb.ap(fc * TH + cb[0], fc * TH + cb[1]), ident_s[:, :]),
                        waits=[(s_ev, vtc_ready)] if fc == 0 else [],
                        inc=(s_mm, 1) if fc == FCD - 1 else None)
                ktr = s_mm.n
                K.op("vector", lambda e: e.tensor_mul(ktd.ap(), psT[:, :], gp_sb.ap()),
                     waits=[(s_mm, ktr)], inc=(s_ev, 1))
                S["psT_free"] = s_ev.n
                ktd_ready = s_ev.n
                for w in range(1, 5):
                    pw_ = w - 1
                    pwb = pw_ % 2
                    for i in range(4):
                        if w < 4:
                            wb = w % 2
                            ex = ([(s_ev, S["sc_free"][wb])] if i == 0 else [])
                            emit_score(w, i, ex)
                        ex = ([(s_ev, mask_done[pwb]), (s_ev, vtc_ready),
                               (s_ev, ktd_ready), (s_ev, qs_ready[cp]),
                               (s_sc, S["rt_free"][pwb]), (s_ev, S["st_free"][pwb]),
                               (s_ev, S["aupd_blk"][pw_])] if i == 0 else [])
                        emit_rcs(pw_, i, ex)
                    sc_done[w % 2] = s_mm.n if w < 4 else sc_done[w % 2]
                    rcs = s_mm.n
                    rcs_done[pwb] = rcs
                    S["smb_free"][pwb] = rcs
                    if w < 4:
                        emit_mask(w)
                    K.op("scalar", lambda e, pwb=pwb, pw_=pw_, cp=cp: e.copy(
                        rttb[cp].ap(256 * pw_, 256 * (pw_ + 1)),
                        rt2[pwb][:, 256 * (pw_ // 2):256 * (pw_ // 2) + 256]),
                        waits=[(s_mm, rcs)] + ([(s_ev, S["rtt_free"][cp])] if pw_ == 0 else []),
                        inc=(s_sc, 1))
                    S["rt_free"][pwb] = s_sc.n
                    K.drain("vector")
                    for bi in range(2):
                        b = 2 * pw_ + bi
                        K.op("vector", lambda e, pwb=pwb, pw_=pw_, b=b, bi=bi:
                             e.scalar_tensor_tensor(
                                 A_fb.ap(64 * b, 64 * b + 64),
                                 A_fb.ap(64 * b, 64 * b + 64),
                                 gcpg_sb.ap(8 + b, 9 + b),
                                 st2[pwb][:, 128 * (pw_ // 2) + 64 * bi:
                                          128 * (pw_ // 2) + 64 * bi + 64],
                                 ALU.mult, ALU.add),
                             waits=[(s_mm, rcs)] if bi == 0 else [])
                    K.drain("vector")
                    K.op("vector", lambda e, pw_=pw_: e.tensor_copy(
                        A_bb.ap(128 * pw_, 128 * (pw_ + 1)),
                        A_fb.ap(128 * pw_, 128 * (pw_ + 1))),
                        inc=(s_ev, 1))
                    S["aupd_blk"][pw_] = s_ev.n
                    S["st_free"][pwb] = s_ev.n
                    if S["pend_vec"]:
                        S["pend_vec"].pop(0)()
                    if w == 3 and lc + 1 < NLC:
                        emit_qsks(lc + 1, 1 - cp)
                while S["pend_vec"]:
                    S["pend_vec"].pop(0)()
                if S["pend_tail"] is not None:
                    S["pend_tail"]()
                    S["pend_tail"] = None

                def gn_s1(cp=cp):
                    K.op("scalar", lambda e: e.square(gsq.ap(), rttb[cp].ap()),
                         waits=[(s_sc, s_sc.n), (s_mm, s_mm.n)])
                    K.drain("scalar", inc=(s_sc, 1))
                    S["gsq_done"] = s_sc.n

                def gn_s2(cp=cp):
                    K.op("vector", lambda e: e.tensor_reduce(
                        gnst.ap(0, 16), rttb[cp].ap().rearrange("p (h d) -> p h d", d=64),
                        AX.X, ALU.add))
                    K.drain("vector")

                def gn_s3(cp=cp):
                    K.op("vector", lambda e: e.tensor_reduce(
                        gnst.ap(16, 32), gsq.ap().rearrange("p (h d) -> p h d", d=64),
                        AX.X, ALU.add), waits=[(s_sc, S["gsq_done"])])
                    K.drain("vector")

                def gn_s4(cp=cp):
                    K.op("vector", lambda e: e.tensor_scalar_mul(
                        gnst.ap(32, 48), gnst.ap(0, 16), cpool[:, 3:4]))
                    K.drain("vector")
                    K.op("vector", lambda e: e.tensor_mul(gnst.ap(48, 64), gnst.ap(32, 48),
                                                          gnst.ap(32, 48)))
                    K.drain("vector")
                    K.op("vector", lambda e: e.scalar_tensor_tensor(
                        gnst.ap(0, 16), gnst.ap(16, 32), cpool[:, 3:4], gnst.ap(48, 64),
                        ALU.mult, ALU.subtract))
                    K.drain("vector", inc=(s_ev, 1))
                    gve = s_ev.n
                    K.op("scalar", lambda e: e.activation(gnst.ap(16, 32), gnst.ap(0, 16),
                                                          AF.Ln, bias=cpool[:, 0:1]),
                         waits=[(s_ev, gve)])
                    K.drain("scalar")
                    K.op("scalar", lambda e: e.mul(gnst.ap(16, 32), gnst.ap(16, 32),
                                                   cpool[:, 1:2]))
                    K.drain("scalar")
                    K.op("scalar", lambda e: e.activation(gnst.ap(16, 32), gnst.ap(16, 32),
                                                          AF.Exp))
                    K.drain("scalar", inc=(s_sc, 1))
                    S["gsc"] = s_sc.n

                def gn_s5(cp=cp):
                    gsc = S["gsc"]
                    for h in range(H):
                        K.op("vector", lambda e, h=h: e.tensor_scalar(
                            gnn.ap(h * 64, (h + 1) * 64), rttb[cp].ap(h * 64, (h + 1) * 64),
                            gnst.ap(32 + h, 33 + h), gnst.ap(16 + h, 17 + h),
                            ALU.subtract, ALU.mult),
                            waits=[(s_sc, gsc)] if h == 0 else [])
                    K.drain("vector", inc=(s_ev, 1))
                    S["rtt_free"][cp] = s_ev.n
                    S["gnn_done"] = s_ev.n

                def gn_vec():
                    for f in (gn_s1, gn_s2, gn_s3, gn_s4, gn_s5):
                        f()

                def gn_tail(cb=cb):
                    gnn_done = S["gnn_done"]
                    for fc in range(FCD):
                        K.op("tensor", lambda e, fc=fc: e.transpose(
                            psT2[:, fc * 128:(fc + 1) * 128],
                            gnn.ap(fc * 128, (fc + 1) * 128), ident_s[:, :]),
                            waits=[(s_ev, gnn_done), (s_sc, S["gn_evac"])] if fc == 0 else [],
                            inc=(s_mm, 1) if fc == FCD - 1 else None)
                    gtr = s_mm.n
                    for fc in range(FCD):
                        K.op("scalar", lambda e, fc=fc: e.activation(
                            gnf.ap(), psT2[:, fc * 128:(fc + 1) * 128], AF.Identity,
                            bias=vp("gnb", fc), scale=vp("gnw", fc)),
                            waits=[(s_mm, gtr), (s_ev, s_ev.n)] if fc == 0
                            else [(s_ev, s_ev.n)],
                            inc=(s_sc, 1))
                        gfa = s_sc.n
                        K.op("vector", lambda e, fc=fc, cb=cb: e.tensor_mul(
                            gb.ap(fc * TH + cb[0], fc * TH + cb[1]),
                            gb.ap(fc * TH + cb[0], fc * TH + cb[1]), gnf.ap()),
                            waits=[(s_sc, gfa)], inc=(s_ev, 1))
                    S["gn_evac"] = s_sc.n

                S["pend_vec"] = [gn_s1, gn_s2, gn_s3, gn_s4, gn_s5]
                S["pend_tail"] = gn_tail
            while S["pend_vec"]:
                S["pend_vec"].pop(0)()
            if S["pend_tail"] is not None:
                S["pend_tail"]()
                S["pend_tail"] = None
            K.op("tensor", lambda e: e.nop(), waits=[(s_sc, s_sc.n), (s_ev, s_ev.n),
                                                     (s_gp, s_gp.n)])
            for b in (qsb[0], qsb[1], ktd, vtc, smb[0], smb[1],
                      rttb[0], rttb[1], gsq, gnn, gnf, gnst):
                b.free()
        qb.free()
        kb.free()
        vb.free()
        K.barrier(f"ret{hh}")
        if hh == 0 and "p1" in tap_ap:
            tap_gate()
            for fc in range(FCD):
                K.op("sync", lambda e, fc=fc: e.dma_start(
                    out=tap_ap["p1"][fc * 128:(fc + 1) * 128, 0:TH],
                    in_=gb.ap(fc * TH, (fc + 1) * TH)), inc=(s_store, 16))
            tap_sync()
        m1sb = AF32.alloc(TH, "m1sb")
        r1sb = AF32.alloc(TH, "r1sb")
        yws = AF32.alloc(TH, "yws")
        zws = [AF32.alloc(TH, f"zw{i}") for i in range(2)]
        zbf = AB.alloc(TH, "zbf")
        zsq = AB.alloc(TH, "zsq")
        strb[0] = AF32.alloc(TH, "wo_s0")
        strb[1] = AF32.alloc(TH, "wo_s1")
        with contextlib.ExitStack() as ph:
            p_ = ph.enter_context
            with contextlib.ExitStack() as ph2:
                psb2 = ph2.enter_context(nc.psum_tensor(f"psb2_{hh}", [128, TH], F32))
                bcast_row(RW_M1, psb2, (t0, t1))
                K.op("scalar", lambda e: e.copy(m1sb.ap(), psb2[:, :]),
                     waits=[(s_mm, s_mm.n)], inc=(s_sc, 1))
                bcast_row(RW_R1, psb2, (t0, t1))
                K.op("scalar", lambda e: e.copy(r1sb.ap(), psb2[:, :]),
                     waits=[(s_mm, s_mm.n)], inc=(s_sc, 1))
                K.op("tensor", lambda e: e.nop(), waits=[(s_sc, s_sc.n)])
            pstz = p_(nc.psum_tensor(f"pstz_{hh}", [1, TH], F32))
            pstv2 = p_(nc.psum_tensor(f"pstv2_{hh}", [1, TH], F32))
            psG = [p_(nc.psum_tensor(f"psGo{i}_{hh}", [128, TH], F32)) for i in range(2)]

            def oevac(oc, ps, mmw):
                sl = oc % 2
                wv = stream_rows(xT, oc, sl, (t0, t1))
                bb = strb[sl]
                spw = [(s_sp[sl], 16 * sp_rnd[sl])] if sp_rnd[sl] > 0 else []
                K.op("vector", lambda e, bb=bb: e.tensor_sub(
                    yws.ap(), bb.ap(), m1sb.ap()),
                    waits=[(s_str[sl], wv), (s_sc, s_sc.n), (s_ev, s_ev.n)],
                    inc=(s_stru[sl], 1))
                K.drain("vector")
                K.op("vector", lambda e: e.tensor_mul(yws.ap(), yws.ap(), r1sb.ap()))
                K.drain("vector")
                K.op("vector", lambda e, oc=oc, ps=ps, sl=sl: e.scalar_tensor_tensor(
                    zws[sl].ap(), ps[:, :], vp("bo", oc), yws.ap(), ALU.add, ALU.add),
                    waits=[mmw] + spw, inc=(s_ev, 1))
                evd = s_ev.n
                spill(z_scr, oc, zws[sl].ap(), sl, (t0, t1), [(s_ev, evd)])
                K.op("scalar", lambda e, sl=sl: e.copy(zbf.ap(), zws[sl].ap()),
                     waits=[(s_ev, evd), (s_mm, s_mm.n)])
                K.drain("scalar", inc=(s_sc, 1))
                cpd = s_sc.n
                K.op("vector", lambda e: e.tensor_mul(zsq.ap(), zbf.ap(), zbf.ap()),
                     waits=[(s_sc, cpd)])
                K.drain("vector", inc=(s_ev, 1))
                sqd = s_ev.n
                for ts in range(2):
                    mm(pstz[0:1, ts * 512:(ts + 1) * 512], ones_bf[:, 0:1],
                       zbf.ap(ts * 512, (ts + 1) * 512), oc == 0, oc == FCD - 1,
                       waits=[(s_sc, cpd)] if ts == 0 else [])
                for ts in range(2):
                    mm(pstv2[0:1, ts * 512:(ts + 1) * 512], ones_bf[:, 0:1],
                       zsq.ap(ts * 512, (ts + 1) * 512), oc == 0, oc == FCD - 1,
                       waits=[(s_ev, sqd)] if ts == 0 else [],
                       inc=(s_mm, 1) if ts == 1 else None)
                return (s_ev, evd)

            ws_layer("wo", wo_p, FCD, FCD,
                     lambda kc, ts: gb.ap(kc * TH + ts * 512, kc * TH + (ts + 1) * 512),
                     oevac, psG, (t0, t1))
            for b in (m1sb, r1sb, yws):
                b.free()
            row_math(pstz, pstv2, (RW_MZ, RW_SZ), False, f"zst{hh}", (t0, t1),
                     wait0=[(s_mm, s_mm.n)])
            K.op("sync", lambda e: e.nop(),
                 waits=[(s_sp[0], 16 * sp_rnd[0]), (s_sp[1], 16 * sp_rnd[1])])
        for b in (zws[0], zws[1], zbf, zsq, strb[0], strb[1]):
            b.free()
        strb[0] = strb[1] = None
        gb.free()
        K.barrier(f"wo{hh}")

    def phaseC_half(hh, x2, ug_job=None):
        t0, t1 = hh * TH, (hh + 1) * TH
        if hh == 0 and "x2" in tap_ap:
            tap_gate()
            for fc in range(FCD):
                K.op("sync", lambda e, fc=fc: e.dma_start(
                    out=tap_ap["x2"][fc * 128:(fc + 1) * 128, 0:TH],
                    in_=x2.ap(fc * TH, (fc + 1) * TH)), inc=(s_store, 16))
            tap_sync()
        pmat = AB.alloc(KCI * TH, "pmat")
        usil = [AB.alloc(TH, f"usil{i}") for i in range(2)]
        with contextlib.ExitStack() as ph:
            p_ = ph.enter_context
            psG = [p_(nc.psum_tensor(f"psGc{i}_{hh}", [128, TH], F32)) for i in range(2)]
            grp_free = [None] * 2
            G = PANW // D
            ngrp = (KCI + G - 1) // G
            gwj = {0: [], 1: []}
            K.job = ug_job
            for j, panels in ((0, win_u), (1, win_g)):
                for gi in range(min(2, ngrp)):
                    gwj[j].append(panel_load(panels, gi * G, j + 2 * gi, min(G, KCI - gi * G), D))
            for oc in range(KCI):
                for j, panels in ((0, win_u), (1, win_g)):
                    gi = oc // G
                    slot = j + 2 * (gi % 2)
                    off = (oc % G) * D
                    pg = j
                    waits = []
                    if oc % G == 0:
                        waits.append((s_pan[slot], gwj[j][gi]))
                    if grp_free[pg] is not None:
                        waits.append(grp_free[pg])
                    first = True
                    grp_last = (oc % G == G - 1) or (oc == KCI - 1)
                    for kc in range(FCD):
                        for ts in range(TH // 512):
                            last = (kc == FCD - 1 and ts == TH // 512 - 1)
                            inc = None
                            if last:
                                inc = (s_panu[slot], 1) if grp_last else (s_mm, 1)
                            mm(psG[pg][:, ts * 512:(ts + 1) * 512],
                               pan[slot][:, off + kc * 128:off + (kc + 1) * 128],
                               x2.ap(kc * TH + ts * 512, kc * TH + (ts + 1) * 512),
                               kc == 0, kc == FCD - 1,
                               waits=waits if first else [],
                               inc=inc)
                            first = False
                    mm_done = ((s_panu[slot], s_panu[slot].n) if grp_last
                               else (s_mm, s_mm.n))
                    if grp_last and gi + 2 < ngrp:
                        gwj[j].append(panel_load(panels, (gi + 2) * G, j + 2 * (gi % 2),
                                                 min(G, KCI - (gi + 2) * G), D))
                    if j == 0:
                        K.op("scalar", lambda e, oc=oc, ps=psG[pg]: e.activation(
                            usil[oc % 2].ap(), ps[:, :], AF.Silu, bias=vp("binu", oc)),
                            waits=[mm_done, (s_ev, s_ev.n)], inc=(s_sc, 1))
                        grp_free[pg] = (s_sc, s_sc.n)
                    else:
                        K.op("vector", lambda e, oc=oc, ps=psG[pg]: e.scalar_tensor_tensor(
                            pmat.ap(oc * TH, (oc + 1) * TH), ps[:, :], vp("bing", oc),
                            usil[oc % 2].ap(), ALU.add, ALU.mult),
                            waits=[mm_done, (s_sc, s_sc.n)], inc=(s_ev, 1))
                        grp_free[pg] = (s_ev, s_ev.n)
                    K.pump()
            K.drain_job()
            K.op("tensor", lambda e: e.nop(), waits=[gf for gf in grp_free if gf is not None])
        x2.free()
        K.barrier(f"pmat{hh}")
        if hh == 0 and "pmat" in tap_ap:
            tap_gate()
            for kc in range(KCI):
                K.op("sync", lambda e, kc=kc: e.dma_start(
                    out=tap_ap["pmat"][kc * 128:(kc + 1) * 128, 0:TH],
                    in_=pmat.ap(kc * TH, (kc + 1) * TH)), inc=(s_store, 16))
            tap_sync()
        tws = [AF32.alloc(TH, f"tw{i}") for i in range(2)]
        x1bf = AB.alloc(TH, "x1bf")
        x1sq = AB.alloc(TH, "x1sq")
        strb[0] = AF32.alloc(TH, "wt_s0")
        strb[1] = AF32.alloc(TH, "wt_s1")
        with contextlib.ExitStack() as ph:
            p_ = ph.enter_context
            pstx = p_(nc.psum_tensor(f"pstx_{hh}", [1, TH], F32))
            pstxv = p_(nc.psum_tensor(f"pstxv_{hh}", [1, TH], F32))
            psGW = [p_(nc.psum_tensor(f"psGw{i}_{hh}", [128, TH], F32)) for i in range(2)]

            def wevac(oc, ps, mmw):
                wz = stream_rows(z_scr, oc, 0, (t0, t1))
                wx = stream_rows(xT, oc, 1, (t0, t1))
                b0, b1 = strb[0], strb[1]
                sl = oc % 2
                spw = [(s_sp[sl], 16 * sp_rnd[sl])] if sp_rnd[sl] > 0 else []
                K.op("vector", lambda e, oc=oc, ps=ps, sl=sl, b0=b0: e.scalar_tensor_tensor(
                    tws[sl].ap(), ps[:, :], vp("b_out", oc), b0.ap(),
                    ALU.add, ALU.add),
                    waits=[mmw, (s_str[0], wz)] + spw, inc=(s_stru[0], 1))
                K.drain("vector")
                K.op("vector", lambda e, sl=sl, b1=b1: e.tensor_add(
                    tws[sl].ap(), tws[sl].ap(), b1.ap()),
                    waits=[(s_str[1], wx)], inc=(s_stru[1], 1))
                K.drain("vector", inc=(s_ev, 1))
                evd = s_ev.n
                spill(x1_scr, oc, tws[sl].ap(), sl, (t0, t1), [(s_ev, evd)])
                K.op("scalar", lambda e, sl=sl: e.copy(x1bf.ap(), tws[sl].ap()),
                     waits=[(s_ev, evd), (s_mm, s_mm.n)])
                K.drain("scalar", inc=(s_sc, 1))
                cpd = s_sc.n
                K.op("vector", lambda e: e.tensor_mul(x1sq.ap(), x1bf.ap(), x1bf.ap()),
                     waits=[(s_sc, cpd)])
                K.drain("vector", inc=(s_ev, 1))
                sqd = s_ev.n
                for ts in range(2):
                    mm(pstx[0:1, ts * 512:(ts + 1) * 512], ones_bf[:, 0:1],
                       x1bf.ap(ts * 512, (ts + 1) * 512), oc == 0, oc == FCD - 1,
                       waits=[(s_sc, cpd)] if ts == 0 else [])
                for ts in range(2):
                    mm(pstxv[0:1, ts * 512:(ts + 1) * 512], ones_bf[:, 0:1],
                       x1sq.ap(ts * 512, (ts + 1) * 512), oc == 0, oc == FCD - 1,
                       waits=[(s_ev, sqd)] if ts == 0 else [],
                       inc=(s_mm, 1) if ts == 1 else None)
                return (s_ev, evd)

            ws_layer("wout", wout_p, KCI, FCD,
                     lambda kc, ts: pmat.ap(kc * TH + ts * 512, kc * TH + (ts + 1) * 512),
                     wevac, psGW, (t0, t1), pan_cols=HPAD)
            row_math(pstx, pstxv, (RW_MY, RW_RY), False, f"x1st{hh}", (t0, t1),
                     wait0=[(s_mm, s_mm.n)])
            K.op("sync", lambda e: e.nop(),
                 waits=[(s_sp[0], 16 * sp_rnd[0]), (s_sp[1], 16 * sp_rnd[1])])
        for b in (pmat, usil[0], usil[1], tws[0], tws[1], x1bf, x1sq,
                  strb[0], strb[1]):
            b.free()
        strb[0] = strb[1] = None
        K.barrier(f"phC{hh}")

    def phaseD_half(hh, y2h, w3_job=None):
        t0, t1 = hh * TH, (hh + 1) * TH
        h1a = AB.alloc(FH * TH, "h1a")
        if hh == 0 and "y2" in tap_ap:
            tap_gate()
            for fc in range(FCD):
                K.op("sync", lambda e, fc=fc: e.dma_start(
                    out=tap_ap["y2"][fc * 128:(fc + 1) * 128, 0:TH],
                    in_=y2h.ap(fc * TH, (fc + 1) * TH)), inc=(s_store, 16))
            tap_sync()
        usil = [AB.alloc(TH, f"usD{i}") for i in range(2)]
        with contextlib.ExitStack() as ph:
            p_ = ph.enter_context
            psH = [p_(nc.psum_tensor(f"psH{i}_{hh}", [128, TH], F32)) for i in range(2)]
            grp_free = [None] * 2
            G = PANW // D
            ngrp = (FH + G - 1) // G
            gwj = {0: [], 1: []}
            for j, panels in ((0, w1_p), (1, w2_p)):
                for gi in range(min(2, ngrp)):
                    gwj[j].append(panel_load(panels, gi * G, j + 2 * gi, G, D))
            for oc in range(FH):
                for j, panels in ((0, w1_p), (1, w2_p)):
                    g = j
                    gi = oc // G
                    slot = j + 2 * (gi % 2)
                    off = (oc % G) * D
                    waits = []
                    if oc % G == 0:
                        waits.append((s_pan[slot], gwj[j][gi]))
                    if grp_free[g] is not None:
                        waits.append(grp_free[g])
                    first = True
                    grp_last = (oc % G == G - 1) or (oc == FH - 1)
                    for kc in range(FCD):
                        for ts in range(2):
                            last = (kc == FCD - 1 and ts == 1)
                            inc = None
                            if last:
                                inc = (s_panu[slot], 1) if grp_last else (s_mm, 1)
                            mm(psH[g][:, ts * 512:(ts + 1) * 512],
                               pan[slot][:, off + kc * 128:off + (kc + 1) * 128],
                               y2h.ap(kc * TH + ts * 512, kc * TH + (ts + 1) * 512),
                               kc == 0, kc == FCD - 1,
                               waits=waits if first else [],
                               inc=inc)
                            first = False
                    mm_done = ((s_panu[slot], s_panu[slot].n) if grp_last
                               else (s_mm, s_mm.n))
                    if grp_last and gi + 2 < ngrp:
                        gwj[j].append(panel_load(panels, (gi + 2) * G, j + 2 * (gi % 2),
                                                 G, D))
                    if j == 0:
                        K.op("scalar", lambda e, oc=oc, g=g: e.activation(
                            usil[oc % 2].ap(), psH[g][:, :], AF.Silu, scale=vp("s1", oc)),
                            waits=[mm_done, (s_ev, s_ev.n)], inc=(s_sc, 1))
                        grp_free[g] = (s_sc, s_sc.n)
                    else:
                        K.op("vector", lambda e, oc=oc, g=g: e.scalar_tensor_tensor(
                            h1a.ap(oc * TH, (oc + 1) * TH), psH[g][:, :], vp("s2", oc),
                            usil[oc % 2].ap(), ALU.mult, ALU.mult),
                            waits=[mm_done, (s_sc, s_sc.n)], inc=(s_ev, 1))
                        grp_free[g] = (s_ev, s_ev.n)
                    K.pump()
            K.drain_job()
            for gf in grp_free:
                K.op("tensor", lambda e: e.nop(), waits=[gf])
            K.barrier(f"h1a_{hh}")
            if hh == 0 and "h1a" in tap_ap:
                tap_gate()
                for kc in range(FH):
                    K.op("sync", lambda e, kc=kc: e.dma_start(
                        out=tap_ap["h1a"][kc * 128:(kc + 1) * 128, 0:TH],
                        in_=h1a.ap(kc * TH, (kc + 1) * TH)), inc=(s_store, 16))
                tap_sync()
            y2h.free()
            usil[0].free()
            usil[1].free()
            h2a = AB.alloc(FH * TH, "h2a")
            outst = [AF32.alloc(TH, f"outst{i}") for i in range(2)]
            strb[0] = AF32.alloc(TH, "d_s0")
            strb[1] = AF32.alloc(TH, "d_s1")

            def gelu_evac(dstb, sname):
                def evac(oc, ps, mmw):
                    K.op("scalar", lambda e, oc=oc, ps=ps, dstb=dstb, sname=sname:
                         e.activation(dstb.ap(oc * TH, (oc + 1) * TH), ps[:, :],
                                      AF.Gelu, scale=vp(sname, oc)),
                         waits=[mmw], inc=(s_sc, 1))
                    return (s_sc, s_sc.n)
                return evac

            def out_evac(oc, ps, mmw):
                sl = oc % 2
                wx = stream_rows(x1_scr, oc, sl, (t0, t1))
                bb = strb[sl]
                spw = [(s_sp[sl], 16 * sp_rnd[sl])] if sp_rnd[sl] > 0 else []
                K.op("vector", lambda e, oc=oc, ps=ps, sl=sl, bb=bb: e.scalar_tensor_tensor(
                    outst[sl].ap(), ps[:, :], vp("s3", oc),
                    bb.ap(), ALU.mult, ALU.add),
                    waits=[mmw, (s_str[sl], wx)] + spw,
                    inc=(s_stru[sl], 1))
                K.drain("vector", inc=(s_ev, 1))
                evd = s_ev.n
                K.op("sync", lambda e, oc=oc, sl=sl, a=t0, b=t1: e.dma_start(
                    out=out_d[oc * 128:(oc + 1) * 128, a:b], in_=outst[sl].ap()),
                    waits=[(s_ev, evd)], inc=(s_sp[sl], 16))
                sp_rnd[sl] += 1
                return (s_ev, evd)

            psH2 = psH
            ws_layer("h1", h1_p, FH, FH,
                     lambda kc, ts: h1a.ap(kc * TH + ts * 512, kc * TH + (ts + 1) * 512),
                     gelu_evac(h2a, "sh1"), psH2, (t0, t1), pan_cols=DH)
            K.barrier(f"h2a_{hh}")
            ws_layer("h2", h2_p, FH, FH,
                     lambda kc, ts: h2a.ap(kc * TH + ts * 512, kc * TH + (ts + 1) * 512),
                     gelu_evac(h1a, "sh2"), psH2, (t0, t1), pan_cols=DH)
            K.barrier(f"h3a_{hh}")
            h2a.free()
            K.job = w3_job
            ws_layer("w3", w3_p, FH, FCD,
                     lambda kc, ts: h1a.ap(kc * TH + ts * 512, kc * TH + (ts + 1) * 512),
                     out_evac, psH2, (t0, t1), pan_cols=DH)
            K.drain_job()
            K.barrier(f"outw_{hh}")
            for b in (outst[0], outst[1], strb[0], strb[1]):
                b.free()
            strb[0] = strb[1] = None
        h1a.free()

    order = ["prefix", "A", "C", "D"]
    stop = len(order) if upto == "full" else order.index(upto) + 1

    boxes = {}

    def lazy_apply(key, src, mr, sr, aff, tag, cols):
        boxes[key] = AB.alloc(FCD * TH, key)
        yield from apply_job(
            src, mr, sr,
            lambda fc, key=key: boxes[key].ap(fc * TH, (fc + 1) * TH),
            aff, tag, cols)

    if stop >= 1:
        prefix_phase(
            pre_kv_job=None,
            state_job=chain_jobs(
                ln_stats_job(xT, (RW_M1, RW_R1, RW_S12), True, "m1"),
                lazy_apply("xnb0", xT, RW_M1, RW_S12,
                           ("n1w", "n1b"), "xn0", (0, TH))) if stop >= 2 else None)
    if stop >= 2:
        mask_sb = AF32.alloc(16 * C, "mask")
        K.op("sync", lambda e: e.dma_start(out=mask_sb.ap(), in_=mask_c[:, :]),
             inc=(s_vload, 16))
        for _e in ("tensor", "vector", "scalar"):
            K.op(_e, lambda en: en.nop(), waits=[(s_vload, s_vload.n)])
        K.barrier("consts")
        attn_half(0, mask_sb, boxes["xnb0"],
                  qkvg_job=lazy_apply("xnb1", xT, RW_M1, RW_S12,
                                      ("n1w", "n1b"), "xn1", (TH, 2 * TH)))
        attn_half(1, mask_sb, boxes["xnb1"],
                  qkvg_job=lazy_apply("x2b0", z_scr, RW_MZ, RW_SZ,
                                      ("n2w", "n2b"), "x2h0", (0, TH)))
        if "Af" in tap_ap:
            tap_gate()
            K.op("sync", lambda e: e.dma_start(out=tap_ap["Af"][:, :], in_=A_f),
                 inc=(s_store, 16))
            tap_sync()
        for b in (mask_sb,):
            b.free()
    if stop >= 3:
        A_fb.free()
        A_bb.free()
        gp_sb.free()
        gq_sb.free()
        gcpg_sb.free()
        phaseC_half(0, boxes["x2b0"],
                    ug_job=lazy_apply("x2b1", z_scr, RW_MZ, RW_SZ,
                                      ("n2w", "n2b"), "x2h1", (TH, 2 * TH)))
        phaseC_half(1, boxes["x2b1"],
                    ug_job=lazy_apply("y2b0", x1_scr, RW_MY, RW_RY,
                                      None, "y2h0", (0, TH)))
    if stop >= 4:
        phaseD_half(0, boxes["y2b0"],
                    w3_job=lazy_apply("y2b1", x1_scr, RW_MY, RW_RY,
                                      None, "y2h1", (TH, 2 * TH)))
        phaseD_half(1, boxes["y2b1"])

    K.op("sync", lambda e: e.nop(),
         waits=[(s_store, s_store.n), (s_sp[0], 16 * sp_rnd[0]), (s_sp[1], 16 * sp_rnd[1]),
                (s_row, s_row.n)])
    K.barrier("end")
    K.finalize()
    es.close()
    return nc


_PREP = {}


def prep_weights(inputs):
    key = id(inputs.get("L0_attn_wq"))
    if _PREP.get("key") == key:
        return _PREP["data"]
    f32 = np.float32
    dmask, ident, selc = host_consts()
    w_in = np.asarray(inputs["L0_attn_w_in"], f32)
    b_in = np.asarray(inputs["L0_attn_b_in"], f32)
    w_out = np.asarray(inputs["L0_attn_w_out"], f32)
    winu = np.zeros((D, HPAD), f32)
    winu[:, :HIN] = w_in[:, :HIN]
    wing = np.zeros((D, HPAD), f32)
    wing[:, :HIN] = w_in[:, HIN:]
    binu = np.zeros(HPAD, f32)
    binu[:HIN] = b_in[:HIN]
    bing = np.zeros(HPAD, f32)
    bing[:HIN] = b_in[HIN:]
    woutp = np.zeros((HPAD, D), f32)
    woutp[:HIN, :] = w_out

    vecs = dict(
        bq=inputs["L0_attn_bq"], bk=inputs["L0_attn_bk"], bv=inputs["L0_attn_bv"],
        bg=inputs["L0_attn_bg"], bo=inputs["L0_attn_bo"],
        n1w=inputs["L0_attn_norm1_w"], n1b=inputs["L0_attn_norm1_b"],
        gnw=inputs["L0_attn_gn_w"], gnb=inputs["L0_attn_gn_b"],
        n2w=inputs["L0_attn_norm2_w"], n2b=inputs["L0_attn_norm2_b"],
        b_out=inputs["L0_attn_b_out"], binu=binu, bing=bing,
        s1=inputs["L0_ffn_w1_s"], s2=inputs["L0_ffn_w2_s"],
        sh1=inputs["L0_ffn_h1_s"], sh2=inputs["L0_ffn_h2_s"], s3=inputs["L0_ffn_w3_s"],
    )
    vpk = np.zeros((128, NVCOL), f32)
    for nme, ncc in zip(VEC_NAMES, VEC_NCH):
        v = np.asarray(vecs[nme], f32).reshape(ncc, 128).T
        vpk[:, VCOL[nme]:VCOL[nme] + ncc] = v

    data = dict(
        wq_p=pack_ws(np.asarray(inputs["L0_attn_wq"], f32)),
        wk_p=pack_ws(np.asarray(inputs["L0_attn_wk"], f32)),
        wv_p=pack_ws(np.asarray(inputs["L0_attn_wv"], f32)),
        wg_p=pack_ws(np.asarray(inputs["L0_attn_wg"], f32)),
        wo_p=pack_ws(np.asarray(inputs["L0_attn_wo"], f32)),
        win_u=pack_ws(winu), win_g=pack_ws(wing),
        wout_p=pack_ws(woutp),
        w1_p=pack_ws(np.asarray(inputs["L0_ffn_w1_w"], f32).T),
        w2_p=pack_ws(np.asarray(inputs["L0_ffn_w2_w"], f32).T),
        h1_p=pack_ws(np.asarray(inputs["L0_ffn_h1_w"], f32).T),
        h2_p=pack_ws(np.asarray(inputs["L0_ffn_h2_w"], f32).T),
        w3_p=pack_ws(np.asarray(inputs["L0_ffn_w3_w"], f32).T),
        vpack=vpk,
        brows=np.stack([np.asarray(inputs["L0_attn_bk"], f32),
                        np.asarray(inputs["L0_attn_bv"], f32)]),
        mask_c=dmask, ident_c=ident, selc_c=selc,
    )
    _PREP["key"] = key
    _PREP["data"] = data
    return data


_NC_CACHE = {}


def get_nc(upto="full", taps=()):
    key = (upto, tuple(sorted(taps)))
    if key not in _NC_CACHE:
        _NC_CACHE[key] = build_program(upto, taps)
    return _NC_CACHE[key]


def make_in_maps(inputs):
    x = np.asarray(inputs["L0_attn_x"], np.float32)
    shared = prep_weights(inputs)
    in_maps = []
    for c in range(8):
        b, s = c // 2, c % 2
        im = dict(shared)
        im["xT"] = np.ascontiguousarray(x[b, s * T:(s + 1) * T, :].T)
        im["xTpre"] = np.ascontiguousarray(x[b, 0:T, :].T)
        im["gate_in"] = np.full(128, float(s), np.float32)
        in_maps.append(im)
    return in_maps


def kernel(**inputs):
    nc = get_nc()
    in_maps = make_in_maps(inputs)
    res = run_bass_kernel_spmd(nc, in_maps, core_ids=list(range(8)))
    odt = np.asarray(res.results[0]["out"]).dtype
    out = np.empty((4, 4096, 1024), odt)
    for c in range(8):
        b, s = c // 2, c % 2
        out[b, s * T:(s + 1) * T, :] = res.results[c]["out"].T
    return out
